# version 30
# baseline (speedup 1.0000x reference)
from contextlib import ExitStack

import numpy as np
import ml_dtypes

import concourse.bass as bass
import concourse.mybir as mybir
from concourse.bass_utils import run_bass_kernel_spmd

F32 = mybir.dt.float32
BF16 = mybir.dt.bfloat16
U32 = mybir.dt.uint32
I32 = mybir.dt.int32
AF = mybir.ActivationFunctionType
ALU = mybir.AluOpType
AX = mybir.AxisListType

NCORES = 8
B_PC = 2
T = 2048
E = 1024
H = 16
D = 64
NHP = 8
C = 128
NCH = T // C
NTT = T // 128
KT = E // 128

MAGIC_RSQRT = 0x5F3759DF
C_MAGIC = 1.5 * 2.0**20
LN_EPS = 1e-5

_CFG = dict(n_batch=B_PC, n_hp=NHP, phase2=True, dumps=(), d_stage=3)
_NC_CACHE = {}


class _Ctr:
    def __init__(self, handle):
        self.h = handle
        self.n = 0

    def inc(self, inst, k=1):
        inst.then_inc(self.h, k)
        self.n += k
        return self.n

    def wait(self, eng, val=None):
        v = self.n if val is None else val
        if v > 0:
            eng.wait_ge(self.h, v)


def build_core():
    cfg = _CFG
    n_batch = cfg["n_batch"]
    n_hp = cfg["n_hp"]
    do_p2 = cfg["phase2"]
    d_stage = cfg.get("d_stage", 3)
    dumps = set(cfg["dumps"])

    nc = bass.Bass("TRN2", target_bir_lowering=False)

    x_in = nc.dram_tensor("x", [B_PC, T, E], F32, kind="ExternalInput").ap()
    y_out = nc.dram_tensor("y", [B_PC, T, E], F32, kind="ExternalOutput").ap()
    Wd = {nm: nc.dram_tensor(nm, [E, E], F32, kind="ExternalInput").ap()
          for nm in ["Wq", "Wk", "Wv", "Wg", "Wf"]}
    bd = {nm: nc.dram_tensor(nm, [E], F32, kind="ExternalInput").ap()
          for nm in ["bq", "bk", "bv", "bg", "bf"]}
    gamma = nc.dram_tensor("gamma", [1, H], F32, kind="ExternalInput").ap()
    cos2 = nc.dram_tensor("cos2", [128, T], BF16, kind="ExternalInput").ap()
    sin2 = nc.dram_tensor("sin2", [128, T], BF16, kind="ExternalInput").ap()
    P2c = nc.dram_tensor("P2c", [128, 128], BF16, kind="ExternalInput").ap()
    identc = nc.dram_tensor("identc", [128, 128], BF16, kind="ExternalInput").ap()
    DTc = nc.dram_tensor("DTc", [128, 128], F32, kind="ExternalInput").ap()
    tauc = nc.dram_tensor("tauc", [128, 128], F32, kind="ExternalInput").ap()
    cvecc = nc.dram_tensor("cvecc", [128, 1], F32, kind="ExternalInput").ap()
    c128c = nc.dram_tensor("c128c", [128, 1], F32, kind="ExternalInput").ap()
    ones1 = nc.dram_tensor("ones1", [1, 128], F32, kind="ExternalInput").ap()
    eye8 = nc.dram_tensor("eye8", [8, 8], F32, kind="ExternalInput").ap()
    WTqk_scr = nc.dram_tensor("WTqk_scr", [2, NHP, 128, E], BF16, kind="Internal").ap()
    WTvg_scr = nc.dram_tensor("WTvg_scr", [2, KT, 128, E], BF16, kind="Internal").ap()

    dump_aps = {}
    for name, shape, dt in [("xnT", [128, KT * T], BF16), ("V", [128, NTT * E], BF16),
                            ("gate", [128, NTT * E], BF16), ("qrot", [128, T], BF16),
                            ("krot", [128, T], BF16), ("attnout", [128, NTT * E], BF16),
                            ("wdT", [128, KT * E], BF16)]:
        if name in dumps:
            dump_aps[name] = nc.dram_tensor("dump_" + name, shape, dt,
                                            kind="ExternalOutput").ap()

    with ExitStack() as st:
        def sb(name, shape, dt):
            return st.enter_context(nc.sbuf_tensor(name, shape, dt))

        def ps(name, shape, dt):
            return st.enter_context(nc.psum_tensor(name, shape, dt))

        Vg = sb("Vg", [128, NTT * E], BF16)
        gate = sb("gate", [128, NTT * E], BF16)
        wdT = sb("wdT", [128, KT * E], BF16)
        Wq_hp = sb("Wq_hp", [128, E], BF16)
        Wk_hp = sb("Wk_hp", [128, E], BF16)
        QbT = sb("QbT", [128, T], BF16)
        QrT = sb("QrT", [128, T], BF16)
        KrT = sb("KrT", [128, T], BF16)
        lamK2 = [sb("lamK_0", [128, 128], BF16), sb("lamK_1", [128, 128], BF16)]
        Qt2 = [sb("Qt_0", [128, 128], BF16), sb("Qt_1", [128, 128], BF16)]
        WS2 = [sb("WS_0", [128, 256], BF16), sb("WS_1", [128, 256], BF16)]
        Zd2 = [sb("Zd_0", [128, D], BF16), sb("Zd_1", [128, D], BF16)]
        S_bfA = sb("S_bfA", [128, D], BF16)
        S_bfB = sb("S_bfB", [128, D], BF16)
        Ztot = sb("Ztot", [128, D], F32)
        Wt_hp = sb("Wt_hp", [128, 256], BF16)
        dmat = sb("dmat", [128, 128], BF16)
        lam_pair = sb("lam_pair", [128, 2], BF16)
        lam_mat = sb("lam_mat", [128, 128], BF16)
        gmix = sb("gmix", [128, 1], F32)
        gamC = sb("gamC", [128, 1], F32)
        cos_sb = sb("cos_sb", [128, T], BF16)
        sin_sb = sb("sin_sb", [128, T], BF16)
        P2_sb = sb("P2_sb", [128, 128], BF16)
        id_sb = sb("id_sb", [128, 128], BF16)
        DT_sb = sb("DT_sb", [128, 128], F32)
        tau_sb = sb("tau_sb", [128, 128], F32)
        cvec_sb = sb("cvec_sb", [128, 1], F32)
        c128_sb = sb("c128_sb", [128, 1], F32)
        ones1_sb = sb("ones1_sb", [1, 128], F32)
        g_row = sb("g_row", [1, H], F32)
        negg_row = sb("negg_row", [1, H], F32)
        negg_rep = sb("negg_rep", [128, H], F32)
        bq_cols = sb("bq_cols", [128, NHP], F32)
        bk_cols = sb("bk_cols", [128, NHP], F32)
        bv_row = sb("bv_row", [1, E], F32)
        bg_row = sb("bg_row", [1, E], F32)
        bf_row = sb("bf_row", [1, E], F32)
        Wot = sb("Wot", [128, E], F32)
        Tstg = sb("Tstg", [128, E], BF16)
        q_amax = sb("q_amax", [128, E // 32], F32)
        q_sneg = sb("q_sneg", [128, E // 32], F32)
        q_spos = sb("q_spos", [128, E // 32], F32)
        q_nrs = sb("q_nrs", [128, E // 32], F32)
        q_y = sb("q_y", [128, E], F32)
        q_m1 = sb("q_m1", [128, E], F32)
        q_t = sb("q_t", [128, E], F32)
        bn_all = sb("bn_all", [128, 12 * 16], F32)
        b8 = bn_all[0:8, 0:128]
        eye8_sb = bn_all[0:8, 128:136]
        st_m = sb("st_m", [128, 16], F32)
        st_dm = sb("st_dm", [128, 16], F32)
        st_v = sb("st_v", [128, 16], F32)
        st_r = sb("st_r", [128, 16], F32)
        st_nm = sb("st_nm", [128, 16], F32)
        st_a = sb("st_a", [128, 16], F32)
        st_h = sb("st_h", [128, 16], F32)
        xn_tm = sb("xn_tm", [128, E], BF16)
        Wc_bf = xn_tm
        Zhi = xn_tm[:, 0:64]
        Zlo = xn_tm[:, 64:128]

        q_qn = q_y
        xt = [Wot, q_m1, q_t, q_y]
        q_xd = Tstg
        rot_pairs = [(Tstg[:, 0:512], Tstg[:, 512:1024]),
                     (q_t[:, 0:256].bitcast(BF16), q_t[:, 256:512].bitcast(BF16))]

        P_S = ps("P_S", [128, 512], F32)
        P_OUT = ps("P_OUT", [128, 512], F32)
        P_T = ps("P_T", [128, 512], F32)
        P_T2 = ps("P_T2", [128, 512], F32)
        PT_bf = P_T[:, 0:64].bitcast(BF16)
        PT2_bf = P_T2[:, 0:64].bitcast(BF16)
        P_Z = ps("P_Z", [128, D], F32)
        P_U = ps("P_U", [128, D], F32)
        P_BIG = ps("P_BIG", [128, 512], F32)
        P_MISC_g = nc.psum_tensor("P_MISC", [128, 16], F32)
        P_MISC = P_MISC_g.__enter__()

        LD_POOL = [_Ctr(st.enter_context(nc.semaphore("s_ld%d" % i)))
                   for i in range(8)]
        ST_POOL = [_Ctr(st.enter_context(nc.semaphore("s_st%d" % i)))
                   for i in range(4)]
        s_pe = _Ctr(st.enter_context(nc.semaphore("s_pe")))
        s_dve = _Ctr(st.enter_context(nc.semaphore("s_dve")))
        s_act = _Ctr(st.enter_context(nc.semaphore("s_act")))
        s_st_gp = _Ctr(st.enter_context(nc.semaphore("s_gp")))
        _rr = [0, 0]

        blk = st.enter_context(nc.Block())

        _keys = {}

        def LD(waits, *dmas, key=None):
            if key is None:
                p = LD_POOL[_rr[0] % len(LD_POOL)]
                _rr[0] += 1
            else:
                if key not in _keys:
                    _keys[key] = LD_POOL[len(_keys) % len(LD_POOL)]
                p = _keys[key]

            @blk.sync
            def _(sync):
                p.wait(sync)
                for cc, vv in waits:
                    cc.wait(sync, vv)
                for out_ap, in_ap in dmas:
                    p.inc(sync.dma_start(out=out_ap, in_=in_ap), 16)
            return (p, p.n)

        def STO(waits, *dmas):
            p = ST_POOL[_rr[1] % len(ST_POOL)]
            _rr[1] += 1

            @blk.gpsimd
            def _(g):
                p.wait(g)
                for cc, vv in waits:
                    cc.wait(g, vv)
                for out_ap, in_ap in dmas:
                    p.inc(g.dma_start(out=out_ap, in_=in_ap), 16)
            return (p, p.n)

        def all_st():
            return [(p, p.n) for p in ST_POOL]

        act_q = []

        def SC(fn, n_inc=1):
            s_act.n += n_inc
            act_q.append(fn)

        def flush_sc():
            if not act_q:
                return
            q = list(act_q)
            del act_q[:]

            @blk.scalar
            def _(sc):
                for fn in q:
                    fn(sc)

        yo_tokens = {}
        consts_ld = LD([], (cos_sb[:], cos2), (sin_sb[:], sin2), (P2_sb[:], P2c),
           (id_sb[:], identc), (DT_sb[:], DTc), (tau_sb[:], tauc),
           (cvec_sb[:], cvecc), (c128_sb[:], c128c), (ones1_sb[:], ones1),
           (eye8_sb, eye8), (g_row[:], gamma),
           (bv_row[:], bd["bv"].rearrange("(a e) -> a e", a=1)),
           (bg_row[:], bd["bg"].rearrange("(a e) -> a e", a=1)),
           (bf_row[:], bd["bf"].rearrange("(a e) -> a e", a=1)))

        def _g(sc, tok=consts_ld):
            tok[0].wait(sc, tok[1])
            sc.activation(out=st_a[0:1, 0:H], in_=g_row[:], func=AF.Exp,
                          scale=-1.0).then_inc(s_act.h, 1)
        SC(_g)

        @blk.vector
        def _(v):
            s_act.wait(v)
            v.tensor_scalar(out=st_h[0:1, 0:H], in0=st_a[0:1, 0:H], scalar1=1.0,
                            scalar2=None, op0=ALU.add)
            v.drain()
            v.reciprocal(out=g_row[:], in_=st_h[0:1, 0:H])
            v.drain()
            s_dve.inc(v.tensor_scalar(out=negg_row[:], in0=g_row[:], scalar1=-1.0,
                            scalar2=None, op0=ALU.mult))

        @blk.tensor
        def _(t):
            s_dve.wait(t)
            s_pe.inc(t.matmul(out=P_MISC[:, 0:H], lhsT=ones1_sb[:], rhs=negg_row[:],
                              start=True, stop=True))

        @blk.vector
        def _(v):
            s_pe.wait(v)
            s_dve.inc(v.tensor_copy(out=negg_rep[:], in_=P_MISC[:, 0:H]))

        for (bias_dram, cols, scale) in ((bd["bq"], bq_cols, 0.125),
                                         (bd["bk"], bk_cols, 1.0)):
            bld = LD([(s_dve, s_dve.n)],
                     (b8, bias_dram.rearrange("(h p) -> h p", p=128)))

            @blk.tensor
            def _(t, bld=bld):
                bld[0].wait(t, bld[1])
                s_dve.wait(t)
                s_pe.inc(t.matmul(out=P_MISC[:, 0:8], lhsT=b8, rhs=eye8_sb,
                                  start=True, stop=True))

            @blk.vector
            def _(v, cols=cols, scale=scale):
                s_pe.wait(v)
                s_dve.inc(v.tensor_scalar(out=cols[:], in0=P_MISC[:, 0:8], scalar1=scale,
                                scalar2=None, op0=ALU.mult))

        def emit_quant_gp(src_f32, dst_bf):
            @blk.vector
            def _(v, src_f32=src_f32):
                v.drain()
                x3 = src_f32.rearrange("p (b k) -> p b k", k=32)
                nb = E // 32
                v.tensor_reduce(out=q_amax[:], in_=x3, axis=AX.X, op=ALU.max,
                                apply_absolute_value=True)
                v.drain()
                v.tensor_scalar(out=q_sneg[:], in0=q_amax[:], scalar1=-1.0 / 7.5,
                                scalar2=-1e-30, op0=ALU.mult, op1=ALU.add)
                v.drain()
                v.tensor_scalar(out=q_spos[:], in0=q_sneg[:], scalar1=-1.0,
                                scalar2=None, op0=ALU.mult)
                v.reciprocal(out=q_nrs[:], in_=q_sneg[:])
                v.drain()
                nrs_b = q_nrs[:].unsqueeze(2).broadcast_to([128, nb, 32])
                v.tensor_tensor(out=q_y[:].rearrange("p (b k) -> p b k", k=32),
                                in0=x3, in1=nrs_b, op=ALU.mult)
                v.drain()
                v.tensor_scalar(out=q_m1[:].bitcast(U32), in0=q_y[:].bitcast(U32),
                                scalar1=0x7F800000, scalar2=None,
                                op0=ALU.bitwise_and)
                v.drain()
                s_dve.inc(v.tensor_scalar(out=q_m1[:], in0=q_m1[:], scalar1=1.0,
                                          scalar2=C_MAGIC, op0=ALU.max,
                                          op1=ALU.mult))

            @blk.gpsimd
            def _(g):
                s_dve.wait(g)
                g.tensor_tensor(out=q_t[:], in0=q_m1[:], in1=q_y[:], op=ALU.add)
                g.drain()
                s_st_gp.inc(g.tensor_tensor(out=q_qn[:], in0=q_m1[:], in1=q_t[:],
                                            op=ALU.subtract))

            @blk.vector
            def _(v):
                s_st_gp.wait(v)
                nb = E // 32
                spos_b = q_spos[:].unsqueeze(2).broadcast_to([128, nb, 32])
                s_dve.inc(v.tensor_tensor(
                    out=dst_bf.rearrange("p (b k) -> p b k", k=32),
                    in0=q_qn[:].rearrange("p (b k) -> p b k", k=32),
                    in1=spos_b, op=ALU.mult))
            return s_dve.n

        def emit_quant(v, src_f32, dst_bf):
            x3 = src_f32.rearrange("p (b k) -> p b k", k=32)
            nb = E // 32
            v.tensor_reduce(out=q_amax[:], in_=x3, axis=AX.X, op=ALU.max,
                            apply_absolute_value=True)
            v.drain()
            v.tensor_scalar(out=q_sneg[:], in0=q_amax[:], scalar1=-1.0 / 7.5,
                            scalar2=-1e-30, op0=ALU.mult, op1=ALU.add)
            v.drain()
            v.tensor_scalar(out=q_spos[:], in0=q_sneg[:], scalar1=-1.0,
                            scalar2=None, op0=ALU.mult)
            v.reciprocal(out=q_nrs[:], in_=q_sneg[:])
            v.drain()
            nrs_b = q_nrs[:].unsqueeze(2).broadcast_to([128, nb, 32])
            s_dve.inc(v.tensor_tensor(out=q_y[:].rearrange("p (b k) -> p b k", k=32),
                            in0=x3, in1=nrs_b, op=ALU.mult))
            v.tensor_scalar(out=q_m1[:].bitcast(U32), in0=q_y[:].bitcast(U32),
                            scalar1=0x7F800000, scalar2=None, op0=ALU.bitwise_and)
            s_dve.inc(v.tensor_scalar(out=q_m1[:], in0=q_m1[:], scalar1=1.0,
                            scalar2=C_MAGIC, op0=ALU.max, op1=ALU.mult))
            s_dve.inc(v.tensor_tensor(out=q_t[:], in0=q_m1[:], in1=q_y[:],
                            op=ALU.add))
            s_dve.inc(v.tensor_tensor(out=q_qn[:], in0=q_m1[:], in1=q_t[:],
                            op=ALU.subtract))
            spos_b = q_spos[:].unsqueeze(2).broadcast_to([128, nb, 32])
            return v.tensor_tensor(out=dst_bf.rearrange("p (b k) -> p b k", k=32),
                            in0=q_qn[:].rearrange("p (b k) -> p b k", k=32),
                            in1=spos_b, op=ALU.mult)

        w_plan = [("Wq", "qk", 0, 0.125, False), ("Wk", "qk", 1, 1.0, False),
                  ("Wv", "vg", 0, 1.0, False), ("Wg", "vg", 1, 1.0, False),
                  ("Wf", "wd", 0, 1.0, True)]
        Wc2_g = nc.sbuf_tensor("Wc2", [128, E], BF16)
        Wc2 = Wc2_g.__enter__()
        Ts2_g = nc.sbuf_tensor("Ts2", [128, E], BF16)
        Ts2 = Ts2_g.__enter__()
        cast_incs = [0, 0]
        tp_pe = [0, 0]
        sto_tok_w = {}
        gidx = 0
        for (wnm, kind, wi, cast_scale, do_quant) in w_plan:
            for ot in range(8):
                par = gidx % 2
                wcb = Wc_bf if par == 0 else Wc2
                tsb = Tstg if par == 0 else Ts2
                if do_quant:
                    wbuf, wkey = Wot, 'wot'
                    dve_war = s_dve.n
                else:
                    wbuf = Wot if par == 0 else q_m1
                    wkey = 'wot' if par == 0 else 'wot2'
                    dve_war = cast_incs[par]
                bld = LD([(s_dve, dve_war)],
                         (wbuf[:], Wd[wnm][ot * 128:(ot + 1) * 128, :]), key=wkey)

                @blk.vector
                def _(v, bld=bld, cast_scale=cast_scale, do_quant=do_quant,
                      wbuf=wbuf, wcb=wcb, pv=tp_pe[par]):
                    bld[0].wait(v, bld[1])
                    s_pe.wait(v, pv)
                    if do_quant:
                        s_dve.inc(emit_quant(v, wbuf[:], wcb[:]))
                    else:
                        s_dve.inc(v.tensor_scalar(out=wcb[:], in0=wbuf[:],
                                                  scalar1=cast_scale,
                                                  scalar2=None, op0=ALU.mult))
                cast_done = s_dve.n
                cast_incs[par] = cast_done

                pe_base = s_pe.n
                act_base = s_act.n

                @blk.tensor
                def _(t, cast_done=cast_done, act_base=act_base, wcb=wcb):
                    s_dve.wait(t, cast_done)
                    s_act.wait(t, act_base)
                    for kt in range(KT):
                        if kt >= 2:
                            s_act.wait(t, act_base + kt - 1)
                        pt = PT_bf if kt % 2 == 0 else PT2_bf
                        s_pe.inc(t.transpose(out=pt,
                                             in_=wcb[:, kt * 128:(kt + 1) * 128],
                                             identity=id_sb[:]))
                tp_pe[par] = s_pe.n

                def _w(sc, kind=kind, ot=ot, pe_base=pe_base, tsb=tsb,
                       tok=sto_tok_w.get(gidx - 2)):
                    if tok is not None:
                        tok[0].wait(sc, tok[1])
                    for kt in range(KT):
                        s_pe.wait(sc, pe_base + kt + 1)
                        pt = PT_bf if kt % 2 == 0 else PT2_bf
                        if kind == "wd":
                            dst = wdT[:, kt * E + ot * 128: kt * E + (ot + 1) * 128]
                        else:
                            dst = tsb[:, kt * 128:(kt + 1) * 128]
                        sc.copy(out=dst, in_=pt).then_inc(s_act.h, 1)
                SC(_w, KT)

                if kind == "qk":
                    sto_tok_w[gidx] = STO([(s_act, s_act.n)],
                                          (WTqk_scr[wi, ot], tsb[:]))
                elif kind == "vg":
                    sto_tok_w[gidx] = STO(
                        [(s_act, s_act.n)],
                        (WTvg_scr[wi][:, :, ot * 128:(ot + 1) * 128]
                         .rearrange("kt p o -> p kt o"),
                         tsb[:].rearrange("p (kt o) -> p kt o", o=128)))
                gidx += 1
        Ts2_g.__exit__(None, None, None)
        Wc2_g.__exit__(None, None, None)
        if "wdT" in dump_aps:
            STO([(s_act, s_act.n)], (dump_aps["wdT"], wdT[:]))
        flush_sc()
        P_MISC_g.__exit__(None, None, None)
        P_OUT2_g = nc.psum_tensor("P_OUT2", [128, 512], F32)
        P_OUT2 = P_OUT2_g.__enter__()

        def emit_bn_var(v, G, n_per_tile):
            n4 = n_per_tile / 4.0
            m = [bn_all[:, k:12 * G:12] for k in (1, 4, 7, 10)]
            M2 = [bn_all[:, k:12 * G:12] for k in (2, 5, 8, 11)]
            g = slice(0, G)
            v.tensor_tensor(out=st_m[:, g], in0=m[0], in1=m[1], op=ALU.add)
            v.tensor_tensor(out=st_dm[:, g], in0=m[2], in1=m[3], op=ALU.add)
            v.tensor_tensor(out=st_a[:, g], in0=m[0], in1=m[0], op=ALU.mult)
            v.tensor_tensor(out=st_h[:, g], in0=m[1], in1=m[1], op=ALU.mult)
            v.tensor_tensor(out=st_v[:, g], in0=M2[0], in1=M2[1], op=ALU.add)
            v.drain()
            v.tensor_tensor(out=st_m[:, g], in0=st_m[:, g], in1=st_dm[:, g],
                            op=ALU.add)
            v.tensor_tensor(out=st_a[:, g], in0=st_a[:, g], in1=st_h[:, g],
                            op=ALU.add)
            v.drain()
            v.tensor_tensor(out=st_h[:, g], in0=m[2], in1=m[2], op=ALU.mult)
            v.tensor_tensor(out=st_dm[:, g], in0=M2[2], in1=M2[3], op=ALU.add)
            v.tensor_scalar(out=st_m[:, g], in0=st_m[:, g], scalar1=0.25,
                            scalar2=None, op0=ALU.mult)
            v.drain()
            v.tensor_tensor(out=st_a[:, g], in0=st_a[:, g], in1=st_h[:, g],
                            op=ALU.add)
            v.tensor_tensor(out=st_v[:, g], in0=st_v[:, g], in1=st_dm[:, g],
                            op=ALU.add)
            v.drain()
            v.tensor_tensor(out=st_h[:, g], in0=m[3], in1=m[3], op=ALU.mult)
            v.tensor_tensor(out=st_dm[:, g], in0=st_m[:, g], in1=st_m[:, g],
                            op=ALU.mult)
            v.drain()
            v.tensor_tensor(out=st_a[:, g], in0=st_a[:, g], in1=st_h[:, g],
                            op=ALU.add)
            v.drain()
            v.scalar_tensor_tensor(out=st_v[:, g], in0=st_a[:, g], scalar=n4,
                                   in1=st_v[:, g], op0=ALU.mult, op1=ALU.add)
            v.drain()
            v.tensor_scalar(out=st_v[:, g], in0=st_v[:, g],
                            scalar1=1.0 / n_per_tile, scalar2=LN_EPS,
                            op0=ALU.mult, op1=ALU.add)
            v.drain()
            return v.tensor_tensor(out=st_v[:, g], in0=st_v[:, g],
                                   in1=st_dm[:, g], op=ALU.subtract)

        def emit_rstd(G):
            def _r(sc, dv=s_dve.n, G=G):
                s_dve.wait(sc, dv)
                sc.activation(out=st_h[:, 0:G], in_=st_v[:, 0:G], func=AF.Ln)
                sc.drain()
                sc.activation(out=st_r[:, 0:G], in_=st_h[:, 0:G], func=AF.Exp,
                              scale=-0.5)
                sc.drain().then_inc(s_act.h, 1)
            SC(_r)

            @blk.vector
            def _(v):
                s_act.wait(v)
                s_dve.inc(v.scalar_tensor_tensor(
                    out=st_nm[:, 0:G], in0=st_m[:, 0:G], scalar=-1.0,
                    in1=st_r[:, 0:G], op0=ALU.mult, op1=ALU.mult))
            return s_dve.n

        for b in range(n_batch):
            xnT_g = nc.sbuf_tensor("xnT_b%d" % b, [128, KT * T], BF16)
            xnT = xnT_g.__enter__()
            WvT_g = nc.sbuf_tensor("WvT_b%d" % b, [128, KT * E], BF16)
            WvT_b = WvT_g.__enter__()
            WgT_g = nc.sbuf_tensor("WgT_b%d" % b, [128, KT * E], BF16)
            WgT_b = WgT_g.__enter__()

            wvg_ld = LD(all_st(),
               (WvT_b[:].rearrange("p (kt o) -> p kt o", o=E),
                WTvg_scr[0].rearrange("kt p o -> p kt o")),
               (WgT_b[:].rearrange("p (kt o) -> p kt o", o=E),
                WTvg_scr[1].rearrange("kt p o -> p kt o")), key='wvg')

            xnT_done = {}
            pa_pe = {}
            bc_ev = {}
            idx_bc = [0]
            dv_b = s_dve.n
            for grp in range(NTT // 4):
                glds = []
                for j in range(4):
                    tt = grp * 4 + j
                    glds.append(LD([(s_dve, s_dve.n), (s_act, s_act.n)],
                                   (xt[j][:], x_in[b, tt * 128:(tt + 1) * 128, :]),
                                   key=('wot' if j == 0 else 'xt%d' % j)))

                @blk.vector
                def _(v, glds=glds):
                    v.drain()
                    for j in range(4):
                        glds[j][0].wait(v, glds[j][1])
                        v.bn_stats(out=bn_all[:, 12 * j:12 * j + 6],
                                   in_=xt[j][:, 0:512])
                        v.bn_stats(out=bn_all[:, 12 * j + 6:12 * j + 12],
                                   in_=xt[j][:, 512:1024])
                    v.drain()
                    s_dve.inc(emit_bn_var(v, 4, E))
                emit_rstd(4)
                rstdA = s_dve.n

                for j in range(4):
                    tt = grp * 4 + j
                    act_b = s_act.n
                    pe_b = s_pe.n

                    xnb = xn_tm if tt % 2 == 0 else Tstg
                    tokw = sto_tok_w.get(30) if (b == 0 and tt == 1) else None

                    def _n(sc, j=j, rstdA=rstdA, xnb=xnb, tok=tokw,
                           pw=pa_pe.get(tt - 2, pe_b)):
                        s_dve.wait(sc, rstdA)
                        s_pe.wait(sc, pw)
                        if tok is not None:
                            tok[0].wait(sc, tok[1])
                        sc.activation(out=xnb[:], in_=xt[j][:], func=AF.Identity,
                                      bias=st_nm[:, j:j + 1],
                                      scale=st_r[:, j:j + 1]).then_inc(s_act.h, 1)
                    SC(_n)

                    @blk.tensor
                    def _(t, act_b=act_b, xnb=xnb):
                        s_act.wait(t, act_b + 1)
                        for kt in range(KT):
                            if kt >= 2:
                                s_act.wait(t, act_b + 1 + kt - 1)
                            pt = PT_bf if kt % 2 == 0 else PT2_bf
                            s_pe.inc(t.transpose(out=pt,
                                                 in_=xnb[:, kt * 128:(kt + 1) * 128],
                                                 identity=id_sb[:]))
                    pa_pe[tt] = s_pe.n

                    def _e(sc, pe_b=pe_b, tt=tt, xnT=xnT):
                        for kt in range(KT):
                            s_pe.wait(sc, pe_b + kt + 1)
                            pt = PT_bf if kt % 2 == 0 else PT2_bf
                            sc.copy(out=xnT[:, kt * T + tt * 128:
                                            kt * T + (tt + 1) * 128],
                                    in_=pt).then_inc(s_act.h, 1)
                    SC(_e, KT)
                    xnT_done[tt] = s_act.n

                bcg = grp - 1
                if bcg >= 0:
                    for j in range(4):
                        tt = bcg * 4 + j
                        for (WT_b, brow, dstbuf, use_silu) in (
                                (WvT_b, bv_row, Vg, False),
                                (WgT_b, bg_row, gate, True)):
                            for half in range(2):
                                pbank = P_BIG if idx_bc[0] % 2 == 0 else P_S

                                @blk.tensor
                                def _(t, WT_b=WT_b, brow=brow, tt=tt, half=half,
                                      pbank=pbank, bw=bc_ev.get(idx_bc[0] - 2, 0),
                                      xd=xnT_done[tt]):
                                    s_act.wait(t, max(bw, xd))
                                    wvg_ld[0].wait(t, wvg_ld[1])
                                    for kt in range(KT):
                                        t.matmul(out=pbank[:],
                                                 lhsT=xnT[:, kt * T + tt * 128:
                                                          kt * T + (tt + 1) * 128],
                                                 rhs=WT_b[:, kt * E + half * 512:
                                                          kt * E + half * 512 + 512],
                                                 start=(kt == 0), stop=False)
                                    s_pe.inc(t.matmul(out=pbank[:], lhsT=ones1_sb[:],
                                                      rhs=brow[:, half * 512:
                                                               half * 512 + 512],
                                                      start=False, stop=True))

                                def _bc(sc, dstbuf=dstbuf, tt=tt, half=half,
                                        use_silu=use_silu, pv=s_pe.n, pbank=pbank,
                                        dv_b=dv_b):
                                    s_pe.wait(sc, pv)
                                    s_dve.wait(sc, dv_b)
                                    fn = AF.Silu if use_silu else AF.Copy
                                    sc.activation(out=dstbuf[:, tt * E + half * 512:
                                                             tt * E + half * 512 + 512],
                                                  in_=pbank[:],
                                                  func=fn).then_inc(s_act.h, 1)
                                SC(_bc)
                                bc_ev[idx_bc[0]] = s_act.n
                                idx_bc[0] += 1
            for bcg in [NTT // 4 - 1]:
                for j in range(4):
                    tt = bcg * 4 + j
                    for (WT_b, brow, dstbuf, use_silu) in (
                            (WvT_b, bv_row, Vg, False),
                            (WgT_b, bg_row, gate, True)):
                        for half in range(2):
                            pbank = P_BIG if idx_bc[0] % 2 == 0 else P_S

                            @blk.tensor
                            def _(t, WT_b=WT_b, brow=brow, tt=tt, half=half,
                                  pbank=pbank, bw=bc_ev.get(idx_bc[0] - 2, 0),
                                  xd=xnT_done[tt]):
                                s_act.wait(t, max(bw, xd))
                                wvg_ld[0].wait(t, wvg_ld[1])
                                for kt in range(KT):
                                    t.matmul(out=pbank[:],
                                             lhsT=xnT[:, kt * T + tt * 128:
                                                      kt * T + (tt + 1) * 128],
                                             rhs=WT_b[:, kt * E + half * 512:
                                                      kt * E + half * 512 + 512],
                                             start=(kt == 0), stop=False)
                                s_pe.inc(t.matmul(out=pbank[:], lhsT=ones1_sb[:],
                                                  rhs=brow[:, half * 512:
                                                           half * 512 + 512],
                                                  start=False, stop=True))

                            def _bc(sc, dstbuf=dstbuf, tt=tt, half=half,
                                    use_silu=use_silu, pv=s_pe.n, pbank=pbank,
                                    dv_b=dv_b):
                                s_pe.wait(sc, pv)
                                s_dve.wait(sc, dv_b)
                                fn = AF.Silu if use_silu else AF.Copy
                                sc.activation(out=dstbuf[:, tt * E + half * 512:
                                                         tt * E + half * 512 + 512],
                                              in_=pbank[:],
                                              func=fn).then_inc(s_act.h, 1)
                            SC(_bc)
                            bc_ev[idx_bc[0]] = s_act.n
                            idx_bc[0] += 1
            flush_sc()
            if "xnT" in dump_aps and b == 0:
                STO([(s_act, s_act.n)], (dump_aps["xnT"], xnT[:]))

            if "V" in dump_aps and b == 0:
                STO([(s_act, s_act.n)], (dump_aps["V"], Vg[:]))
            if "gate" in dump_aps and b == 0:
                STO([(s_act, s_act.n)], (dump_aps["gate"], gate[:]))

            WgT_g.__exit__(None, None, None)
            WvT_g.__exit__(None, None, None)
            hpD_g = []

            def hsb(name, shape, dt):
                gg = nc.sbuf_tensor(name + "_b%d" % b, shape, dt)
                tt_ = gg.__enter__()
                hpD_g.append(gg)
                return tt_

            K_tm2 = [hsb("K_tmA", [128, T], BF16), hsb("K_tmB", [128, T], BF16)]
            QrT2 = [QrT, hsb("QrT_B", [128, T], BF16)]
            KrT2 = [KrT, hsb("KrT_B", [128, T], BF16)]
            Wq2 = [Wq_hp, hsb("Wq_hp_B", [128, E], BF16)]
            Wk2 = [Wk_hp, hsb("Wk_hp_B", [128, E], BF16)]
            Wt2 = [Wt_hp, hsb("Wt_hp_B", [128, 256], BF16)]
            dmat2 = [dmat, hsb("dmat_B", [128, 128], BF16)]
            lam_pair2 = [lam_pair, hsb("lam_pair_B", [128, 2], BF16)]
            lam_mat2 = [lam_mat, hsb("lam_mat_B", [128, 128], BF16)]
            gamC2 = [gamC, hsb("gamC_B", [128, 1], F32)]
            gmix2 = [gmix, hsb("gmix_B", [128, 1], F32)]

            pt_free = {0: (s_act, s_act.n), 1: (s_act, s_act.n)}
            qbT_free = {c: (s_dve, s_dve.n) for c in range(4)}
            wbuf_free = {0: (s_pe, s_pe.n), 1: (s_pe, s_pe.n)}

            gmix_done = {}
            aux_done = {}
            ztn = {}
            init_done = {}

            def emit_gmix(hpn):
                parn = hpn % 2
                ha, hb_ = 2 * hpn, 2 * hpn + 1

                @blk.vector
                def _(v, parn=parn, ha=ha, hb_=hb_,
                      aw=aux_done.get(hpn - 2, 0)):
                    s_act.wait(v, aw)
                    v.tensor_copy(out=gmix2[parn][0:64, :],
                                  in_=negg_rep[0:64, ha:ha + 1])
                    s_dve.inc(v.tensor_copy(out=gmix2[parn][64:128, :],
                                  in_=negg_rep[64:128, hb_:hb_ + 1]))
                gmix_done[hpn] = s_dve.n

            def queue_aux(hpn):
                parn = hpn % 2
                ha, hb_ = 2 * hpn, 2 * hpn + 1

                def _aux(sc, parn=parn, ha=ha, hb_=hb_, gd=gmix_done[hpn]):
                    s_dve.wait(sc, gd)
                    sc.activation(out=dmat2[parn][:], in_=tau_sb[:], func=AF.Exp,
                                  scale=gmix2[parn][:, 0:1])
                    sc.activation(out=gamC2[parn][:], in_=c128_sb[:], func=AF.Exp,
                                  scale=gmix2[parn][:, 0:1])
                    sc.activation(out=lam_pair2[parn][:, 0:1], in_=cvec_sb[:],
                                  func=AF.Exp, scale=negg_rep[:, ha:ha + 1])
                    sc.activation(out=lam_pair2[parn][:, 1:2], in_=cvec_sb[:],
                                  func=AF.Exp, scale=negg_rep[:, hb_:hb_ + 1])
                    sc.activation(out=Wt2[parn][:, 0:128], in_=DT_sb[:], func=AF.Exp,
                                  scale=negg_rep[:, ha:ha + 1])
                    sc.activation(out=Wt2[parn][:, 128:256], in_=DT_sb[:],
                                  func=AF.Exp,
                                  scale=negg_rep[:, hb_:hb_ + 1]).then_inc(s_act.h, 1)
                SC(_aux)
                aux_done[hpn] = s_act.n

            def make_prep(hpn):
                parn = hpn % 2
                wldn = LD([wbuf_free[parn]],
                          (Wq2[parn][:], WTqk_scr[0, hpn]),
                          (Wk2[parn][:], WTqk_scr[1, hpn]), key='whp')
                pe = [[] for _ in range(16)]
                dve = [[] for _ in range(16)]
                sc_s = [[] for _ in range(16)]
                post_pe = []
                post_sc = []
                pq = {}
                pk = {}
                evq = {}
                evk = {}
                swq = {}
                swk = {}
                rotq = {}
                rotk = {}
                pT = {}
                ktm_done = {}
                st8 = {'pk3': None, 'ktm15': None}

                def bankof(c):
                    return P_T if c % 2 == 0 else P_T2

                def mk_proj(c, is_k):
                    def f(t, c=c, is_k=is_k):
                        wldn[0].wait(t, wldn[1])
                        pbank = bankof(c)
                        if not is_k:
                            if c < 2:
                                bf = pt_free[c % 2]
                                bf[0].wait(t, bf[1])
                            else:
                                s_dve.wait(t, rotq[c - 2])
                            qf = qbT_free.get(c)
                            if qf is not None:
                                qf[0].wait(t, qf[1])
                        else:
                            s_dve.wait(t, rotq[c + 2] if c < 2 else rotk[c - 2])
                        Wbuf = Wk2[parn] if is_k else Wq2[parn]
                        for kt in range(KT):
                            mm = t.matmul(out=pbank[:],
                                          lhsT=Wbuf[:, kt * 128:(kt + 1) * 128],
                                          rhs=xnT[:, kt * T + c * 512:
                                                  kt * T + (c + 1) * 512],
                                          start=(kt == 0), stop=(kt == KT - 1))
                        s_pe.inc(mm)
                        (pk if is_k else pq)[c] = s_pe.n
                    return f

                def mk_evac(c, is_k):
                    bcols = bk_cols if is_k else bq_cols

                    def f(sc, c=c, is_k=is_k, bcols=bcols):
                        s_pe.wait(sc, (pk if is_k else pq)[c])
                        if is_k:
                            s_dve.wait(sc, rotq[c])
                        sc.activation(out=QbT[:, c * 512:(c + 1) * 512],
                                      in_=bankof(c)[:], func=AF.Identity,
                                      bias=bcols[:, hpn:hpn + 1]).then_inc(s_act.h, 1)
                    return f

                def reg_evac(c, is_k):
                    SC(mk_evac(c, is_k))
                    (evk if is_k else evq)[c] = s_act.n

                def mk_swap(c, is_k):
                    def f(t, c=c, is_k=is_k):
                        s_act.wait(t, (evk if is_k else evq)[c])
                        s_pe.inc(t.matmul(out=bankof(c)[:], lhsT=P2_sb[:],
                                          rhs=QbT[:, c * 512:(c + 1) * 512],
                                          start=True, stop=True))
                        (swk if is_k else swq)[c] = s_pe.n
                    return f

                def mk_rot(c, is_k):
                    dst = KrT2[parn] if is_k else QrT2[parn]

                    def f(v, c=c, is_k=is_k, dst=dst):
                        s_pe.wait(v, (swk if is_k else swq)[c])
                        r1, r2 = rot_pairs[c % 2]
                        v.tensor_tensor(out=r1, in0=bankof(c)[:],
                                        in1=sin_sb[:, c * 512:(c + 1) * 512],
                                        op=ALU.mult)
                        v.tensor_tensor(out=r2, in0=QbT[:, c * 512:(c + 1) * 512],
                                        in1=cos_sb[:, c * 512:(c + 1) * 512],
                                        op=ALU.mult)
                        s_dve.inc(v.tensor_tensor(out=dst[:, c * 512:(c + 1) * 512],
                                        in0=r1, in1=r2, op=ALU.add))
                        (rotk if is_k else rotq)[c] = s_dve.n
                        if is_k:
                            qbT_free[c] = (s_dve, s_dve.n)
                    return f

                def mk_T(j):
                    def f(t, j=j):
                        if j < 2:
                            s_dve.wait(t, rotk[2 + j])
                        else:
                            s_act.wait(t, ktm_done[j - 2])
                        s_dve.wait(t, rotk[j // 4])
                        pt = PT_bf if j % 2 == 0 else PT2_bf
                        s_pe.inc(t.transpose(out=pt,
                                             in_=KrT2[parn][:, j * 128:(j + 1) * 128],
                                             identity=id_sb[:]))
                        pT[j] = s_pe.n
                    return f

                def mk_ktm(j):
                    def f(sc, j=j):
                        s_pe.wait(sc, pT[j])
                        pt = PT_bf if j % 2 == 0 else PT2_bf
                        sc.copy(out=K_tm2[parn][:, j * 128:(j + 1) * 128],
                                in_=pt).then_inc(s_act.h, 1)
                    return f

                def reg_ktm(j):
                    SC(mk_ktm(j))
                    ktm_done[j] = s_act.n
                    if j == 15:
                        pt_free[1] = (s_act, s_act.n)
                    if j == 14:
                        pt_free[0] = (s_act, s_act.n)

                pe[0].append(mk_proj(0, False))
                pe[1] += [mk_proj(1, False), mk_swap(0, False)]
                pe[2] += [mk_proj(2, False), mk_swap(1, False)]
                pe[3] += [mk_proj(3, False), mk_swap(2, False)]
                pe[4] += [mk_proj(0, True), mk_swap(3, False)]
                pe[5] += [mk_proj(1, True), mk_swap(0, True)]
                pe[6] += [mk_proj(2, True), mk_swap(1, True)]
                pe[7] += [mk_proj(3, True), mk_swap(2, True)]
                pe[8].append(mk_swap(3, True))
                for j in range(14):
                    pe[9 + j // 2].append(mk_T(j))
                post_pe += [mk_T(14), mk_T(15)]
                for c in range(4):
                    dve[1 + c].append(mk_rot(c, False))
                    dve[5 + c].append(mk_rot(c, True))
                sc_s[0].append(('aux', None))
                for c in range(4):
                    sc_s[c].append(('evac', (c, False)))
                    sc_s[4 + c].append(('evac', (c, True)))
                for j in range(14):
                    sc_s[9 + j // 2].append(('ktm', j))
                post_sc += [('ktm', 14), ('ktm', 15)]

                def run_sc(item):
                    kind, arg = item
                    if kind == 'aux':
                        queue_aux(hpn)
                    elif kind == 'evac':
                        reg_evac(*arg)
                    else:
                        reg_ktm(arg)

                def finish():
                    wbuf_free[parn] = (s_pe, pk[3])

                return dict(pe=pe, dve=dve, sc=sc_s, post_pe=post_pe,
                            post_sc=post_sc, run_sc=run_sc, finish=finish,
                            ktm_done=ktm_done, parn=parn)

            def emit_inter(hpn, prep):
                parn = hpn % 2

                @blk.vector
                def _(v):
                    s_dve.inc(v.memset(P_Z[:], 0.0))
                mz = s_dve.n
                if hpn + 1 < n_hp:
                    emit_gmix(hpn + 1)

                @blk.tensor
                def _(t, mz=mz, parn=parn, kd=prep['ktm_done'][15],
                      ha=2 * hpn, hb_=2 * hpn + 1):
                    t.wait_ge(s_dve.h, mz)
                    s_act.wait(t, kd)
                    for j in range(NCH):
                        t.matmul(out=P_Z[0:64, :],
                                 lhsT=K_tm2[parn][:, j * 128:j * 128 + 64],
                                 rhs=Vg[:, j * E + ha * 64:j * E + ha * 64 + 64],
                                 start=False, stop=False, skip_group_check=True,
                                 tile_position=(0, 0))
                        mm = t.matmul(out=P_Z[64:128, :],
                                      lhsT=K_tm2[parn][:, j * 128 + 64:(j + 1) * 128],
                                      rhs=Vg[:, j * E + hb_ * 64:j * E + hb_ * 64 + 64],
                                      start=False, stop=False, skip_group_check=True,
                                      tile_position=(0, 64))
                    s_pe.inc(mm)
                peY = s_pe.n

                def _zt(sc, peY=peY):
                    s_pe.wait(sc, peY)
                    sc.activation(out=Ztot[:], in_=P_Z[:], func=AF.Identity,
                                  scale=-1.0)
                    sc.activation(out=Zhi[:], in_=P_Z[:], func=AF.Identity,
                                  scale=-1.0).then_inc(s_act.h, 1)
                SC(_zt)
                ztn[hpn] = s_act.n

                @blk.vector
                def _(v, parn=parn, zt=ztn[hpn], ad=aux_done[hpn]):
                    v.drain()
                    s_act.wait(v, zt)
                    s_act.wait(v, ad)
                    v.memset(S_bfA[:], 0.0)
                    lam_b = lam_pair2[parn][:].unsqueeze(2).broadcast_to([128, 2, 64])
                    v.tensor_copy(out=lam_mat2[parn][:].rearrange(
                        "p (h d) -> p h d", d=64), in_=lam_b)
                    v.tensor_tensor(out=lamK2[0][:], in0=K_tm2[parn][:, 0:128],
                                    in1=lam_mat2[parn][:], op=ALU.mult)
                    v.tensor_tensor(out=Qt2[0][:], in0=QrT2[parn][:, 0:128],
                                    in1=dmat2[parn][:], op=ALU.mult)
                    s_dve.inc(v.tensor_tensor(out=Zlo[:], in0=Ztot[:],
                                              in1=Zhi[:], op=ALU.subtract))
                init_done[hpn] = s_dve.n

                @blk.tensor
                def _(t, idn=init_done[hpn]):
                    t.wait_ge(s_dve.h, idn)
                    t.matmul(out=P_Z[:], lhsT=id_sb[:], rhs=Zhi[:],
                             start=True, stop=False, skip_group_check=True)
                    t.matmul(out=P_Z[:], lhsT=id_sb[:], rhs=Zlo[:],
                             start=False, stop=False, skip_group_check=True)

            def chunk_weave(hp, prep_next):
                par = hp % 2
                ha, hb_ = 2 * hp, 2 * hp + 1
                K_tm_c, QrT_c = K_tm2[par], QrT2[par]
                KrT_c, Wt_c, gamC_c = KrT2[par], Wt2[par], gamC2[par]
                dmat_c, lam_mat_c = dmat2[par], lam_mat2[par]
                pe_st = {}
                pe_y = {}
                pe_u = {}
                pe_out = {}
                dWS = {}
                dSu = {}
                dTail = {}
                zd_val = {}

                for i in range(NCH):
                    o = i * 128
                    soff = 0 if i % 2 == 0 else 256
                    wv = dict(i=i, o=o, soff=soff)

                    @blk.tensor
                    def _(t, i=i, o=o, soff=soff, idn=init_done[hp],
                          zt=ztn[hp], pe_list=(prep_next['pe'][i]
                                               if prep_next else [])):
                        if i == 0:
                            t.wait_ge(s_dve.h, idn)
                            s_act.wait(t, zt)
                        else:
                            t.wait_ge(s_dve.h, dWS[i - 1])
                        t.matmul(out=P_S[:, soff:soff + 128],
                                 lhsT=KrT_c[0:64, o:o + 128],
                                 rhs=QrT_c[0:64, o:o + 128], start=True, stop=True,
                                 tile_position=(0, 0))
                        s_pe.inc(t.matmul(out=P_BIG[:, soff:soff + 128],
                                 lhsT=KrT_c[64:128, o:o + 128],
                                 rhs=QrT_c[64:128, o:o + 128], start=True, stop=True,
                                 tile_position=(64, 0)))
                        pe_st[i] = s_pe.n
                        if i >= 1:
                            t.wait_ge(s_act.h, zd_val[i - 1])
                        t.matmul(out=P_Z[0:64, :], lhsT=K_tm_c[:, o:o + 64],
                                 rhs=Vg[:, i * E + ha * 64:i * E + ha * 64 + 64],
                                 start=False, stop=False, skip_group_check=True,
                                 tile_position=(0, 0))
                        s_pe.inc(t.matmul(out=P_Z[64:128, :],
                                 lhsT=K_tm_c[:, o + 64:o + 128],
                                 rhs=Vg[:, i * E + hb_ * 64:i * E + hb_ * 64 + 64],
                                 start=False, stop=False, skip_group_check=True,
                                 tile_position=(0, 64)))
                        pe_y[i] = s_pe.n
                        if i >= 1:
                            t.wait_ge(s_dve.h, dSu[i - 1])
                            t.wait_ge(s_dve.h, dTail[i - 1])
                        t.matmul(out=P_U[0:64, :], lhsT=lamK2[i % 2][:, 0:64],
                                 rhs=Vg[:, i * E + ha * 64:i * E + ha * 64 + 64],
                                 start=True, stop=True, tile_position=(0, 0))
                        s_pe.inc(t.matmul(out=P_U[64:128, :],
                                 lhsT=lamK2[i % 2][:, 64:128],
                                 rhs=Vg[:, i * E + hb_ * 64:i * E + hb_ * 64 + 64],
                                 start=True, stop=True, tile_position=(0, 64)))
                        pe_u[i] = s_pe.n
                        for f in pe_list:
                            f(t)
                        if i >= 1:
                            j = i - 1
                            t.wait_ge(s_dve.h, dTail[j])
                            S_cur = S_bfA if j % 2 == 0 else S_bfB
                            WSc, Qtc = WS2[j % 2], Qt2[j % 2]
                            jo = j * 128
                            t.matmul(out=P_OUT[:, 0:64], lhsT=WSc[:, 0:128],
                                     rhs=Vg[:, j * E + ha * 64:j * E + ha * 64 + 64],
                                     start=True, stop=False)
                            t.matmul(out=P_OUT[:, 0:64], lhsT=Qtc[0:64, :],
                                     rhs=S_cur[0:64, :],
                                     start=False, stop=(j == NCH - 1),
                                     tile_position=(0, 0))
                            if j < NCH - 1:
                                t.matmul(out=P_OUT[:, 0:64],
                                         lhsT=QrT_c[0:64, jo:jo + 128],
                                         rhs=Zd2[j % 2][0:64, :],
                                         start=False, stop=True,
                                         tile_position=(0, 0))
                            t.matmul(out=P_OUT2[:, 0:64], lhsT=WSc[:, 128:256],
                                     rhs=Vg[:, j * E + hb_ * 64:j * E + hb_ * 64 + 64],
                                     start=True, stop=False)
                            t.matmul(out=P_OUT2[:, 0:64], lhsT=Qtc[64:128, :],
                                     rhs=S_cur[64:128, :],
                                     start=False, stop=(j == NCH - 1),
                                     tile_position=(64, 0))
                            if j < NCH - 1:
                                s_pe.inc(t.matmul(out=P_OUT2[:, 0:64],
                                         lhsT=QrT_c[64:128, jo:jo + 128],
                                         rhs=Zd2[j % 2][64:128, :],
                                         start=False, stop=True,
                                         tile_position=(64, 0)))
                            else:
                                s_pe.inc(t.nop())
                            pe_out[j] = s_pe.n

                    def _zd(sc, i=i, py=pe_y[i]):
                        s_pe.wait(sc, py)
                        sc.activation(out=Zd2[i % 2][:], in_=P_Z[:],
                                      func=AF.Identity,
                                      scale=-1.0).then_inc(s_act.h, 1)
                    SC(_zd)
                    zd_val[i] = s_act.n
                    if prep_next:
                        for item in prep_next['sc'][i]:
                            prep_next['run_sc'](item)

                    @blk.vector
                    def _(v, i=i, soff=soff,
                          dve_list=(prep_next['dve'][i] if prep_next else [])):
                        if i == 0:
                            v.drain()
                        s_pe.wait(v, pe_st[i])
                        v.tensor_tensor(out=WS2[i % 2][:, 0:128],
                                        in0=P_S[:, soff:soff + 128],
                                        in1=Wt_c[:, 0:128], op=ALU.mult)
                        s_dve.inc(v.tensor_tensor(out=WS2[i % 2][:, 128:256],
                                        in0=P_BIG[:, soff:soff + 128],
                                        in1=Wt_c[:, 128:256], op=ALU.mult))
                        dWS[i] = s_dve.n
                        for f in dve_list:
                            f(v)
                        s_pe.wait(v, pe_u[i])
                        if i >= 1:
                            s_pe.wait(v, pe_out[i - 1])
                        S_cur = S_bfA if i % 2 == 0 else S_bfB
                        S_nxt = S_bfB if i % 2 == 0 else S_bfA
                        s_dve.inc(v.scalar_tensor_tensor(out=S_nxt[:], in0=S_cur[:],
                                               scalar=gamC_c[:, 0:1], in1=P_U[:],
                                               op0=ALU.mult, op1=ALU.add))
                        dSu[i] = s_dve.n
                        last = None
                        if i >= 1:
                            j = i - 1
                            last = v.tensor_tensor(
                                out=Vg[:, j * E + ha * 64:j * E + ha * 64 + 64],
                                in0=P_OUT[:, 0:64],
                                in1=gate[:, j * E + ha * 64:j * E + ha * 64 + 64],
                                op=ALU.mult)
                            last = v.tensor_tensor(
                                out=Vg[:, j * E + hb_ * 64:j * E + hb_ * 64 + 64],
                                in0=P_OUT2[:, 0:64],
                                in1=gate[:, j * E + hb_ * 64:j * E + hb_ * 64 + 64],
                                op=ALU.mult)
                        if i < NCH - 1:
                            o2 = (i + 1) * 128
                            v.tensor_tensor(out=lamK2[(i + 1) % 2][:],
                                            in0=K_tm_c[:, o2:o2 + 128],
                                            in1=lam_mat_c[:], op=ALU.mult)
                            last = v.tensor_tensor(out=Qt2[(i + 1) % 2][:],
                                            in0=QrT_c[:, o2:o2 + 128],
                                            in1=dmat_c[:], op=ALU.mult)
                        s_dve.inc(last)
                        dTail[i] = s_dve.n

                @blk.tensor
                def _(t, post=(prep_next['post_pe'] if prep_next else [])):
                    j = NCH - 1
                    jo = j * 128
                    t.wait_ge(s_act.h, zd_val[j])
                    t.wait_ge(s_dve.h, dTail[j])
                    S_cur = S_bfA if j % 2 == 0 else S_bfB
                    WSc, Qtc = WS2[j % 2], Qt2[j % 2]
                    t.matmul(out=P_OUT[:, 0:64], lhsT=WSc[:, 0:128],
                             rhs=Vg[:, j * E + ha * 64:j * E + ha * 64 + 64],
                             start=True, stop=False)
                    t.matmul(out=P_OUT[:, 0:64], lhsT=Qtc[0:64, :],
                             rhs=S_cur[0:64, :], start=False, stop=True,
                             tile_position=(0, 0))
                    t.matmul(out=P_OUT2[:, 0:64], lhsT=WSc[:, 128:256],
                             rhs=Vg[:, j * E + hb_ * 64:j * E + hb_ * 64 + 64],
                             start=True, stop=False)
                    s_pe.inc(t.matmul(out=P_OUT2[:, 0:64], lhsT=Qtc[64:128, :],
                             rhs=S_cur[64:128, :], start=False, stop=True,
                             tile_position=(64, 0)))
                    pe_out[NCH - 1] = s_pe.n
                    for f in post:
                        f(t)
                if prep_next:
                    for item in prep_next['post_sc']:
                        prep_next['run_sc'](item)

                @blk.vector
                def _(v):
                    j = NCH - 1
                    s_pe.wait(v, pe_out[j])
                    v.tensor_tensor(
                        out=Vg[:, j * E + ha * 64:j * E + ha * 64 + 64],
                        in0=P_OUT[:, 0:64],
                        in1=gate[:, j * E + ha * 64:j * E + ha * 64 + 64],
                        op=ALU.mult)
                    s_dve.inc(v.tensor_tensor(
                        out=Vg[:, j * E + hb_ * 64:j * E + hb_ * 64 + 64],
                        in0=P_OUT2[:, 0:64],
                        in1=gate[:, j * E + hb_ * 64:j * E + hb_ * 64 + 64],
                        op=ALU.mult))
                if prep_next:
                    prep_next['finish']()

            emit_gmix(0)
            prep = make_prep(0)
            order = []
            for s in range(16):
                order.append(('sc', prep['sc'][s]))
                order.append(('pe', prep['pe'][s]))
                order.append(('dve', prep['dve'][s]))
            for kind, items in order:
                for it in items:
                    if kind == 'sc':
                        prep['run_sc'](it)
                    elif kind == 'pe':
                        @blk.tensor
                        def _(t, it=it):
                            it(t)
                    else:
                        @blk.vector
                        def _(v, it=it):
                            v.drain()
                            it(v)
            for it in prep['post_pe']:
                @blk.tensor
                def _(t, it=it):
                    it(t)
            for item in prep['post_sc']:
                prep['run_sc'](item)
            prep['finish']()

            emit_inter(0, prep)
            for hp in range(n_hp):
                prep_next = make_prep(hp + 1) if hp + 1 < n_hp else None
                chunk_weave(hp, prep_next)
                if prep_next:
                    emit_inter(hp + 1, prep_next)

            for gg in reversed(hpD_g):
                gg.__exit__(None, None, None)

            if "attnout" in dump_aps and b == 0:
                STO([(s_dve, s_dve.n)], (dump_aps["attnout"], Vg[:]))

            xnT_g.__exit__(None, None, None)

            if not do_p2:
                continue

            p2_g = []

            def p2sb(name, shape, dt):
                gg = nc.sbuf_tensor(name + "_b%d" % b, shape, dt)
                tt_ = gg.__enter__()
                p2_g.append(gg)
                return tt_

            ffn_t = [p2sb("ffn_%d" % j, [128, E], F32) for j in range(5)]
            xg_t = [p2sb("xg_%d" % j, [128, E], F32) for j in range(2)]
            o1tmp = [p2sb("o1t_%d" % j, [128, E], F32) for j in range(2)]
            xdT_t = p2sb("xdT", [128, E], BF16)
            yo_t = [p2sb("yo_%d" % j, [128, E], F32) for j in range(2)]
            r2_r = p2sb("r2_r", [128, 16], F32)
            r2_nm = p2sb("r2_nm", [128, 16], F32)
            st_r4 = p2sb("st_r4", [128, 16], F32)
            st_nm4 = p2sb("st_nm4", [128, 16], F32)
            out1_all = gate

            def emit_bn_var2(v, G, n_per_tile, ib, so):
                n4 = n_per_tile / 4.0
                m = [bn_all[:, 12 * ib + k:12 * (ib + G):12] for k in (1, 4, 7, 10)]
                M2 = [bn_all[:, 12 * ib + k:12 * (ib + G):12] for k in (2, 5, 8, 11)]
                g = slice(so, so + G)
                v.tensor_tensor(out=st_m[:, g], in0=m[0], in1=m[1], op=ALU.add)
                v.tensor_tensor(out=st_dm[:, g], in0=m[2], in1=m[3], op=ALU.add)
                v.tensor_tensor(out=st_a[:, g], in0=m[0], in1=m[0], op=ALU.mult)
                v.tensor_tensor(out=st_h[:, g], in0=m[1], in1=m[1], op=ALU.mult)
                v.tensor_tensor(out=st_v[:, g], in0=M2[0], in1=M2[1], op=ALU.add)
                v.drain()
                v.tensor_tensor(out=st_m[:, g], in0=st_m[:, g], in1=st_dm[:, g],
                                op=ALU.add)
                v.tensor_tensor(out=st_a[:, g], in0=st_a[:, g], in1=st_h[:, g],
                                op=ALU.add)
                v.drain()
                v.tensor_tensor(out=st_h[:, g], in0=m[2], in1=m[2], op=ALU.mult)
                v.tensor_tensor(out=st_dm[:, g], in0=M2[2], in1=M2[3], op=ALU.add)
                v.tensor_scalar(out=st_m[:, g], in0=st_m[:, g], scalar1=0.25,
                                scalar2=None, op0=ALU.mult)
                v.drain()
                v.tensor_tensor(out=st_a[:, g], in0=st_a[:, g], in1=st_h[:, g],
                                op=ALU.add)
                v.tensor_tensor(out=st_v[:, g], in0=st_v[:, g], in1=st_dm[:, g],
                                op=ALU.add)
                v.drain()
                v.tensor_tensor(out=st_h[:, g], in0=m[3], in1=m[3], op=ALU.mult)
                v.tensor_tensor(out=st_dm[:, g], in0=st_m[:, g], in1=st_m[:, g],
                                op=ALU.mult)
                v.drain()
                v.tensor_tensor(out=st_a[:, g], in0=st_a[:, g], in1=st_h[:, g],
                                op=ALU.add)
                v.drain()
                v.scalar_tensor_tensor(out=st_v[:, g], in0=st_a[:, g], scalar=n4,
                                       in1=st_v[:, g], op0=ALU.mult, op1=ALU.add)
                v.drain()
                v.tensor_scalar(out=st_v[:, g], in0=st_v[:, g],
                                scalar1=1.0 / n_per_tile, scalar2=LN_EPS,
                                op0=ALU.mult, op1=ALU.add)
                v.drain()
                return v.tensor_tensor(out=st_v[:, g], in0=st_v[:, g],
                                       in1=st_dm[:, g], op=ALU.subtract)

            def emit_rstd2(so, do, dr, dnm):
                def _r(sc, dv=s_dve.n):
                    s_dve.wait(sc, dv)
                    sc.activation(out=st_h[:, so:so + 4], in_=st_v[:, so:so + 4],
                                  func=AF.Ln)
                    sc.drain()
                    sc.activation(out=dr[:, do:do + 4], in_=st_h[:, so:so + 4],
                                  func=AF.Exp, scale=-0.5)
                    sc.drain().then_inc(s_act.h, 1)
                SC(_r)

                @blk.vector
                def _(v, av=s_act.n):
                    s_act.wait(v, av)
                    s_dve.inc(v.scalar_tensor_tensor(
                        out=dnm[:, do:do + 4], in0=st_m[:, so:so + 4], scalar=-1.0,
                        in1=dr[:, do:do + 4], op0=ALU.mult, op1=ALU.mult))
                return s_dve.n

            @blk.vector
            def _(v):
                v.drain()
                for tt in range(NTT):
                    v.bn_stats(out=bn_all[:, 12 * tt:12 * tt + 6],
                               in_=Vg[:, tt * E:tt * E + 512])
                    v.bn_stats(out=bn_all[:, 12 * tt + 6:12 * tt + 12],
                               in_=Vg[:, tt * E + 512:(tt + 1) * E])
                v.drain()
                s_dve.inc(emit_bn_var(v, 16, E))
            emit_rstd(16)

            @blk.vector
            def _(v):
                v.drain()
                v.tensor_copy(out=r2_r[:], in_=st_r[:])
                s_dve.inc(v.tensor_copy(out=r2_nm[:], in_=st_nm[:]))

            p1_add = {}
            r3neg = {}
            r4neg = {}
            quantY = {}
            xd_done = {}
            pe_tp = {-1: s_pe.n}
            evx = {-1: s_act.n}
            ffn_ev2 = {}
            ffn_pe = {}
            FBANKS = [P_BIG, P_S, P_OUT, P_OUT2]
            sto_tok = {}
            dv_ffn0 = s_dve.n

            def emit_p1(tt):
                g = tt // 4
                xld = LD([(s_dve, max(p1_add.get(tt - 2, 0), dv_ffn0))],
                         (xg_t[tt % 2][:], x_in[b, tt * 128:(tt + 1) * 128, :]),
                         key='xg%d' % (tt % 2))

                @blk.vector
                def _(v, tt=tt, g=g, xld=xld):
                    xld[0].wait(v, xld[1])
                    v.scalar_tensor_tensor(out=o1tmp[tt % 2][:],
                                           in0=Vg[:, tt * E:(tt + 1) * E],
                                           scalar=r2_r[:, tt:tt + 1],
                                           in1=xg_t[tt % 2][:],
                                           op0=ALU.mult, op1=ALU.add)
                    s_dve.inc(v.tensor_scalar(
                        out=out1_all[:, tt * E:(tt + 1) * E],
                        in0=o1tmp[tt % 2][:], scalar1=r2_nm[:, tt:tt + 1],
                        scalar2=None, op0=ALU.add))
                    p1_add[tt] = s_dve.n
                    v.drain()
                    sl = 12 * (tt % 8)
                    v.bn_stats(out=bn_all[:, sl:sl + 6],
                               in_=out1_all[:, tt * E:tt * E + 512])
                    s_dve.inc(v.bn_stats(out=bn_all[:, sl + 6:sl + 12],
                               in_=out1_all[:, tt * E + 512:(tt + 1) * E]))
                    if tt % 4 == 3:
                        v.drain()
                        s_dve.inc(emit_bn_var2(v, 4, E, 4 * (g % 2), 4 * (g % 2)))
                if tt % 4 == 3:
                    r3neg[g] = emit_rstd2(4 * (g % 2), 4 * g, st_r, st_nm)

            def emit_p2x(tt):
                @blk.vector
                def _(v, tt=tt):
                    v.tensor_scalar(out=Wot[:],
                                    in0=out1_all[:, tt * E:(tt + 1) * E],
                                    scalar1=st_r[:, tt:tt + 1],
                                    scalar2=st_nm[:, tt:tt + 1],
                                    op0=ALU.mult, op1=ALU.add)

            def emit_p2(tt):
                g = tt // 4

                @blk.vector
                def _(v, tt=tt, pv=pe_tp[tt - 1]):
                    s_pe.wait(v, pv)
                    v.drain()
                    qb = s_dve.n
                    s_dve.inc(emit_quant(v, Wot[:], q_xd[:]))
                    quantY[tt] = qb + 1
                xd_done[tt] = s_dve.n

                evbase = s_act.n

                def _xe(sc, tt=tt):
                    pv = pe_tp[tt] - KT
                    for kt in range(KT):
                        s_pe.wait(sc, pv + kt + 1)
                        pt = PT_bf if kt % 2 == 0 else PT2_bf
                        sc.copy(out=xdT_t[:, kt * 128:(kt + 1) * 128],
                                in_=pt).then_inc(s_act.h, 1)
                SC(_xe, KT)
                evx[tt] = s_act.n

                @blk.tensor
                def _(t, tt=tt, xd=xd_done[tt], ev=evbase):
                    t.wait_ge(s_dve.h, xd)
                    s_act.wait(t, ev)
                    for kt in range(KT):
                        if kt >= 2:
                            s_act.wait(t, ev + kt - 1)
                        pt = PT_bf if kt % 2 == 0 else PT2_bf
                        s_pe.inc(t.transpose(out=pt,
                                             in_=q_xd[:, kt * 128:(kt + 1) * 128],
                                             identity=id_sb[:]))
                pe_tp[tt] = s_pe.n

                for half in range(2):
                    pbank = FBANKS[(2 * tt + half) % 4]

                    @blk.tensor
                    def _(t, half=half, pbank=pbank, av=evx[tt],
                          fv=ffn_ev2.get((tt - 2, half), 0)):
                        s_act.wait(t, max(av - (1 if half == 1 else 0), fv))
                        if tt <= 1:
                            t.wait_ge(s_dve.h, dv_ffn0)
                        for kt in range(KT):
                            t.matmul(out=pbank[:],
                                     lhsT=xdT_t[:, kt * 128:(kt + 1) * 128],
                                     rhs=wdT[:, kt * E + half * 512:
                                             kt * E + half * 512 + 512],
                                     start=(kt == 0), stop=False)
                        s_pe.inc(t.matmul(out=pbank[:], lhsT=ones1_sb[:],
                                          rhs=bf_row[:, half * 512:half * 512 + 512],
                                          start=False, stop=True))
                    ffn_pe[(tt, half)] = s_pe.n

            def emit_p2b(tt):
                g = tt // 4
                for half in range(2):
                    def _fe(sc, tt=tt, half=half, pv=ffn_pe[(tt, half)],
                            pbank=FBANKS[(2 * tt + half) % 4]):
                        s_pe.wait(sc, pv)
                        sc.copy(out=ffn_t[tt % 5][:, half * 512:half * 512 + 512],
                                in_=pbank[:]).then_inc(s_act.h, 1)
                    SC(_fe)
                    ffn_ev2[(tt, half)] = s_act.n

                @blk.vector
                def _(v, tt=tt, g=g, fv=ffn_ev2[(tt, 1)]):
                    s_act.wait(v, fv)
                    sl = 12 * (8 + tt % 8)
                    v.bn_stats(out=bn_all[:, sl:sl + 6],
                               in_=ffn_t[tt % 5][:, 0:512])
                    s_dve.inc(v.bn_stats(out=bn_all[:, sl + 6:sl + 12],
                               in_=ffn_t[tt % 5][:, 512:1024]))
                    if tt % 4 == 3:
                        v.drain()
                        s_dve.inc(emit_bn_var2(v, 4, E, 8 + 4 * (g % 2),
                                               8 + 4 * (g % 2)))
                if tt % 4 == 3:
                    r4neg[g] = emit_rstd2(8 + 4 * (g % 2), 4 * g, st_r4, st_nm4)

            def emit_p3(tt):
                g = tt // 4

                def _yo(sc, tt=tt, r4=r4neg[g], tok=sto_tok.get(tt - 2)):
                    s_dve.wait(sc, r4)
                    if tok is not None:
                        tok[0].wait(sc, tok[1])
                    sc.activation(out=yo_t[tt % 2][:], in_=ffn_t[tt % 5][:],
                                  func=AF.Identity, scale=st_r4[:, tt:tt + 1],
                                  bias=st_nm4[:, tt:tt + 1]).then_inc(s_act.h, 1)
                SC(_yo)
                yoa = s_act.n

                @blk.vector
                def _(v, tt=tt, yoa=yoa):
                    s_act.wait(v, yoa)
                    s_dve.inc(v.tensor_tensor(out=yo_t[tt % 2][:],
                                    in0=yo_t[tt % 2][:],
                                    in1=out1_all[:, tt * E:(tt + 1) * E],
                                    op=ALU.add))
                sto_tok[tt] = STO([(s_dve, s_dve.n)],
                                  (y_out[b, tt * 128:(tt + 1) * 128, :],
                                   yo_t[tt % 2][:]))

            LAG2, LAG2B, LAG3 = 4, 6, 9
            for step in range(NTT + LAG3):
                if step < NTT:
                    emit_p1(step)
                if 0 <= step - LAG2B < NTT:
                    emit_p2b(step - LAG2B)
                if 0 <= step - LAG3 < NTT:
                    emit_p3(step - LAG3)
                if 0 <= step - LAG2 < NTT:
                    emit_p2x(step - LAG2)
                    emit_p2(step - LAG2)
            flush_sc()

            for gg in reversed(p2_g):
                gg.__exit__(None, None, None)

        flush_sc()

        @blk.gpsimd
        def _(g):
            for p in ST_POOL:
                p.wait(g)
        P_OUT2_g.__exit__(None, None, None)

    return nc, dump_aps


def _host_consts():
    fi = 1.0 / 10000.0 ** (np.arange(0, D, 2, dtype=np.float64) / D)
    t = np.arange(T, dtype=np.float64)
    p = np.arange(128)
    ang = t[None, :] * fi[p % 32][:, None]
    cos2 = np.cos(ang).astype(ml_dtypes.bfloat16)
    sin2 = np.sin(ang).astype(ml_dtypes.bfloat16)
    P2 = np.zeros((128, 128), dtype=np.float32)
    for i in range(64):
        P2[2 * i + 1, 2 * i] = -1.0
        P2[2 * i, 2 * i + 1] = 1.0
    DT = np.maximum(np.arange(128)[None, :] - np.arange(128)[:, None], 0).astype(np.float32)
    tau = np.broadcast_to(np.arange(128, dtype=np.float32)[None, :], (128, 128)).copy()
    cvec = (float(C) - np.arange(128, dtype=np.float32))[:, None].copy()
    return dict(cos2=cos2, sin2=sin2, P2c=P2.astype(ml_dtypes.bfloat16),
                identc=np.eye(128, dtype=ml_dtypes.bfloat16), DTc=DT, tauc=tau,
                cvecc=cvec, c128c=np.full((128, 1), float(C), np.float32),
                ones1=np.ones((1, 128), np.float32),
                eye8=np.eye(8, dtype=np.float32))


def _make_base_inputs(L0_attn_Wq, L0_attn_bq, L0_attn_Wk, L0_attn_bk, L0_attn_Wv,
                      L0_attn_bv, L0_attn_Wg, L0_attn_bg, L0_attn_gamma,
                      L0_ffn_weight, L0_ffn_bias):
    base = dict(
        Wq=np.ascontiguousarray(L0_attn_Wq, np.float32),
        Wk=np.ascontiguousarray(L0_attn_Wk, np.float32),
        Wv=np.ascontiguousarray(L0_attn_Wv, np.float32),
        Wg=np.ascontiguousarray(L0_attn_Wg, np.float32),
        Wf=np.ascontiguousarray(L0_ffn_weight, np.float32),
        bq=np.ascontiguousarray(L0_attn_bq, np.float32),
        bk=np.ascontiguousarray(L0_attn_bk, np.float32),
        bv=np.ascontiguousarray(L0_attn_bv, np.float32),
        bg=np.ascontiguousarray(L0_attn_bg, np.float32),
        bf=np.ascontiguousarray(L0_ffn_bias, np.float32),
        gamma=np.ascontiguousarray(np.asarray(L0_attn_gamma, np.float32).reshape(1, H)),
    )
    base.update(_host_consts())
    return base


def kernel(L0_attn_x, L0_attn_Wq, L0_attn_bq, L0_attn_Wk, L0_attn_bk,
           L0_attn_Wv, L0_attn_bv, L0_attn_Wg, L0_attn_bg, L0_attn_gamma,
           L0_ffn_weight, L0_ffn_bias):
    x = np.ascontiguousarray(np.asarray(L0_attn_x, dtype=np.float32))
    if "full" not in _NC_CACHE:
        _NC_CACHE["full"] = build_core()
    nc, _ = _NC_CACHE["full"]
    base = _make_base_inputs(L0_attn_Wq, L0_attn_bq, L0_attn_Wk, L0_attn_bk,
                             L0_attn_Wv, L0_attn_bv, L0_attn_Wg, L0_attn_bg,
                             L0_attn_gamma, L0_ffn_weight, L0_ffn_bias)
    in_maps = []
    for c in range(NCORES):
        m = dict(base)
        m["x"] = x[c * B_PC:(c + 1) * B_PC]
        in_maps.append(m)
    res = run_bass_kernel_spmd(nc, in_maps, core_ids=list(range(NCORES)))
    out = np.empty((NCORES * B_PC, T, E), np.float32)
    for c in range(NCORES):
        out[c * B_PC:(c + 1) * B_PC] = np.asarray(res.results[c]["y"])
    return out


# version 31
# speedup vs baseline: 1.0149x; 1.0149x over previous
from contextlib import ExitStack

import numpy as np
import ml_dtypes

import concourse.bass as bass
import concourse.mybir as mybir
from concourse.bass_utils import run_bass_kernel_spmd

F32 = mybir.dt.float32
BF16 = mybir.dt.bfloat16
U32 = mybir.dt.uint32
I32 = mybir.dt.int32
AF = mybir.ActivationFunctionType
ALU = mybir.AluOpType
AX = mybir.AxisListType

NCORES = 8
B_PC = 2
T = 2048
E = 1024
H = 16
D = 64
NHP = 8
C = 128
NCH = T // C
NTT = T // 128
KT = E // 128

MAGIC_RSQRT = 0x5F3759DF
C_MAGIC = 1.5 * 2.0**20
LN_EPS = 1e-5

_CFG = dict(n_batch=B_PC, n_hp=NHP, phase2=True, dumps=(), d_stage=3)
_NC_CACHE = {}


class _Ctr:
    def __init__(self, handle):
        self.h = handle
        self.n = 0

    def inc(self, inst, k=1):
        inst.then_inc(self.h, k)
        self.n += k
        return self.n

    def wait(self, eng, val=None):
        v = self.n if val is None else val
        if v > 0:
            eng.wait_ge(self.h, v)


def build_core():
    cfg = _CFG
    n_batch = cfg["n_batch"]
    n_hp = cfg["n_hp"]
    do_p2 = cfg["phase2"]
    d_stage = cfg.get("d_stage", 3)
    dumps = set(cfg["dumps"])

    nc = bass.Bass("TRN2", target_bir_lowering=False)

    x_in = nc.dram_tensor("x", [B_PC, T, E], F32, kind="ExternalInput").ap()
    y_out = nc.dram_tensor("y", [B_PC, T, E], F32, kind="ExternalOutput").ap()
    Wd = {nm: nc.dram_tensor(nm, [E, E], F32, kind="ExternalInput").ap()
          for nm in ["Wq", "Wk", "Wv", "Wg", "Wf"]}
    bd = {nm: nc.dram_tensor(nm, [E], F32, kind="ExternalInput").ap()
          for nm in ["bq", "bk", "bv", "bg", "bf"]}
    gamma = nc.dram_tensor("gamma", [1, H], F32, kind="ExternalInput").ap()
    cos2 = nc.dram_tensor("cos2", [128, T], BF16, kind="ExternalInput").ap()
    sin2 = nc.dram_tensor("sin2", [128, T], BF16, kind="ExternalInput").ap()
    P2c = nc.dram_tensor("P2c", [128, 128], BF16, kind="ExternalInput").ap()
    identc = nc.dram_tensor("identc", [128, 128], BF16, kind="ExternalInput").ap()
    DTc = nc.dram_tensor("DTc", [128, 128], F32, kind="ExternalInput").ap()
    tauc = nc.dram_tensor("tauc", [128, 128], F32, kind="ExternalInput").ap()
    cvecc = nc.dram_tensor("cvecc", [128, 1], F32, kind="ExternalInput").ap()
    c128c = nc.dram_tensor("c128c", [128, 1], F32, kind="ExternalInput").ap()
    ones1 = nc.dram_tensor("ones1", [1, 128], F32, kind="ExternalInput").ap()
    eye8 = nc.dram_tensor("eye8", [8, 8], F32, kind="ExternalInput").ap()
    WTqk_scr = nc.dram_tensor("WTqk_scr", [2, NHP, 128, E], BF16, kind="Internal").ap()
    WTvg_scr = nc.dram_tensor("WTvg_scr", [2, KT, 128, E], BF16, kind="Internal").ap()

    dump_aps = {}
    for name, shape, dt in [("xnT", [128, KT * T], BF16), ("V", [128, NTT * E], BF16),
                            ("gate", [128, NTT * E], BF16), ("qrot", [128, T], BF16),
                            ("krot", [128, T], BF16), ("attnout", [128, NTT * E], BF16),
                            ("wdT", [128, KT * E], BF16)]:
        if name in dumps:
            dump_aps[name] = nc.dram_tensor("dump_" + name, shape, dt,
                                            kind="ExternalOutput").ap()

    with ExitStack() as st:
        def sb(name, shape, dt):
            return st.enter_context(nc.sbuf_tensor(name, shape, dt))

        def ps(name, shape, dt):
            return st.enter_context(nc.psum_tensor(name, shape, dt))

        Vg = sb("Vg", [128, NTT * E], BF16)
        gate = sb("gate", [128, NTT * E], BF16)
        wdT = sb("wdT", [128, KT * E], BF16)
        Wq_hp = sb("Wq_hp", [128, E], BF16)
        Wk_hp = sb("Wk_hp", [128, E], BF16)
        QbT = sb("QbT", [128, T], BF16)
        QrT = sb("QrT", [128, T], BF16)
        KrT = sb("KrT", [128, T], BF16)
        lamK2 = [sb("lamK_0", [128, 128], BF16), sb("lamK_1", [128, 128], BF16)]
        Qt2 = [sb("Qt_0", [128, 128], BF16), sb("Qt_1", [128, 128], BF16)]
        WS2 = [sb("WS_0", [128, 256], BF16), sb("WS_1", [128, 256], BF16)]
        Zd2 = [sb("Zd_0", [128, D], BF16), sb("Zd_1", [128, D], BF16)]
        S_bfA = sb("S_bfA", [128, D], BF16)
        S_bfB = sb("S_bfB", [128, D], BF16)
        Ztot = sb("Ztot", [128, D], F32)
        Wt_hp = sb("Wt_hp", [128, 256], BF16)
        dmat = sb("dmat", [128, 128], BF16)
        lam_pair = sb("lam_pair", [128, 2], BF16)
        lam_mat = sb("lam_mat", [128, 128], BF16)
        gmix = sb("gmix", [128, 1], F32)
        gamC = sb("gamC", [128, 1], F32)
        cos_sb = sb("cos_sb", [128, T], BF16)
        sin_sb = sb("sin_sb", [128, T], BF16)
        P2_sb = sb("P2_sb", [128, 128], BF16)
        id_sb = sb("id_sb", [128, 128], BF16)
        DT_sb = sb("DT_sb", [128, 128], F32)
        tau_sb = sb("tau_sb", [128, 128], F32)
        cvec_sb = sb("cvec_sb", [128, 1], F32)
        c128_sb = sb("c128_sb", [128, 1], F32)
        ones1_sb = sb("ones1_sb", [1, 128], F32)
        g_row = sb("g_row", [1, H], F32)
        negg_row = sb("negg_row", [1, H], F32)
        negg_rep = sb("negg_rep", [128, H], F32)
        bq_cols = sb("bq_cols", [128, NHP], F32)
        bk_cols = sb("bk_cols", [128, NHP], F32)
        bv_row = sb("bv_row", [1, E], F32)
        bg_row = sb("bg_row", [1, E], F32)
        bf_row = sb("bf_row", [1, E], F32)
        Wot = sb("Wot", [128, E], F32)
        Tstg = sb("Tstg", [128, E], BF16)
        q_amax = sb("q_amax", [128, E // 32], F32)
        q_sneg = sb("q_sneg", [128, E // 32], F32)
        q_spos = sb("q_spos", [128, E // 32], F32)
        q_nrs = sb("q_nrs", [128, E // 32], F32)
        q_y = sb("q_y", [128, E], F32)
        q_m1 = sb("q_m1", [128, E], F32)
        q_t = sb("q_t", [128, E], F32)
        bn_all = sb("bn_all", [128, 12 * 16], F32)
        b8 = bn_all[0:8, 0:128]
        eye8_sb = bn_all[0:8, 128:136]
        st_m = sb("st_m", [128, 16], F32)
        st_dm = sb("st_dm", [128, 16], F32)
        st_v = sb("st_v", [128, 16], F32)
        st_r = sb("st_r", [128, 16], F32)
        st_nm = sb("st_nm", [128, 16], F32)
        st_a = sb("st_a", [128, 16], F32)
        st_h = sb("st_h", [128, 16], F32)
        xn_tm = sb("xn_tm", [128, E], BF16)
        Wc_bf = xn_tm
        Zhi = xn_tm[:, 0:64]
        Zlo = xn_tm[:, 64:128]

        q_qn = q_y
        xt = [Wot, q_m1, q_t, q_y]
        q_xd = Tstg
        rot_pairs = [(Tstg[:, 0:512], Tstg[:, 512:1024]),
                     (q_t[:, 0:256].bitcast(BF16), q_t[:, 256:512].bitcast(BF16))]

        P_S = ps("P_S", [128, 512], F32)
        P_OUT = ps("P_OUT", [128, 512], F32)
        P_T = ps("P_T", [128, 512], F32)
        P_T2 = ps("P_T2", [128, 512], F32)
        PT_bf = P_T[:, 0:64].bitcast(BF16)
        PT2_bf = P_T2[:, 0:64].bitcast(BF16)
        P_Z = ps("P_Z", [128, D], F32)
        P_U = ps("P_U", [128, D], F32)
        P_BIG = ps("P_BIG", [128, 512], F32)
        P_MISC_g = nc.psum_tensor("P_MISC", [128, 16], F32)
        P_MISC = P_MISC_g.__enter__()

        LD_POOL = [_Ctr(st.enter_context(nc.semaphore("s_ld%d" % i)))
                   for i in range(8)]
        ST_POOL = [_Ctr(st.enter_context(nc.semaphore("s_st%d" % i)))
                   for i in range(4)]
        s_pe = _Ctr(st.enter_context(nc.semaphore("s_pe")))
        s_dve = _Ctr(st.enter_context(nc.semaphore("s_dve")))
        s_act = _Ctr(st.enter_context(nc.semaphore("s_act")))
        s_st_gp = _Ctr(st.enter_context(nc.semaphore("s_gp")))
        _rr = [0, 0]

        blk = st.enter_context(nc.Block())

        _keys = {}

        def LD(waits, *dmas, key=None):
            if key is None:
                p = LD_POOL[_rr[0] % len(LD_POOL)]
                _rr[0] += 1
            else:
                if key not in _keys:
                    _keys[key] = LD_POOL[len(_keys) % len(LD_POOL)]
                p = _keys[key]

            @blk.sync
            def _(sync):
                p.wait(sync)
                for cc, vv in waits:
                    cc.wait(sync, vv)
                for out_ap, in_ap in dmas:
                    p.inc(sync.dma_start(out=out_ap, in_=in_ap), 16)
            return (p, p.n)

        def STO(waits, *dmas):
            p = ST_POOL[_rr[1] % len(ST_POOL)]
            _rr[1] += 1

            @blk.gpsimd
            def _(g):
                p.wait(g)
                for cc, vv in waits:
                    cc.wait(g, vv)
                for out_ap, in_ap in dmas:
                    p.inc(g.dma_start(out=out_ap, in_=in_ap), 16)
            return (p, p.n)

        def all_st():
            return [(p, p.n) for p in ST_POOL]

        act_q = []

        def SC(fn, n_inc=1):
            s_act.n += n_inc
            act_q.append(fn)

        def flush_sc():
            if not act_q:
                return
            q = list(act_q)
            del act_q[:]

            @blk.scalar
            def _(sc):
                for fn in q:
                    fn(sc)

        yo_tokens = {}
        consts_ld = LD([], (cos_sb[:], cos2), (sin_sb[:], sin2), (P2_sb[:], P2c),
           (id_sb[:], identc), (DT_sb[:], DTc), (tau_sb[:], tauc),
           (cvec_sb[:], cvecc), (c128_sb[:], c128c), (ones1_sb[:], ones1),
           (eye8_sb, eye8), (g_row[:], gamma),
           (bv_row[:], bd["bv"].rearrange("(a e) -> a e", a=1)),
           (bg_row[:], bd["bg"].rearrange("(a e) -> a e", a=1)),
           (bf_row[:], bd["bf"].rearrange("(a e) -> a e", a=1)))

        def _g(sc, tok=consts_ld):
            tok[0].wait(sc, tok[1])
            sc.activation(out=st_a[0:1, 0:H], in_=g_row[:], func=AF.Exp,
                          scale=-1.0).then_inc(s_act.h, 1)
        SC(_g)

        @blk.vector
        def _(v):
            s_act.wait(v)
            v.tensor_scalar(out=st_h[0:1, 0:H], in0=st_a[0:1, 0:H], scalar1=1.0,
                            scalar2=None, op0=ALU.add)
            v.drain()
            v.reciprocal(out=g_row[:], in_=st_h[0:1, 0:H])
            v.drain()
            s_dve.inc(v.tensor_scalar(out=negg_row[:], in0=g_row[:], scalar1=-1.0,
                            scalar2=None, op0=ALU.mult))

        @blk.tensor
        def _(t):
            s_dve.wait(t)
            s_pe.inc(t.matmul(out=P_MISC[:, 0:H], lhsT=ones1_sb[:], rhs=negg_row[:],
                              start=True, stop=True))

        @blk.vector
        def _(v):
            s_pe.wait(v)
            s_dve.inc(v.tensor_copy(out=negg_rep[:], in_=P_MISC[:, 0:H]))

        for (bias_dram, cols, scale) in ((bd["bq"], bq_cols, 0.125),
                                         (bd["bk"], bk_cols, 1.0)):
            bld = LD([(s_dve, s_dve.n)],
                     (b8, bias_dram.rearrange("(h p) -> h p", p=128)))

            @blk.tensor
            def _(t, bld=bld):
                bld[0].wait(t, bld[1])
                s_dve.wait(t)
                s_pe.inc(t.matmul(out=P_MISC[:, 0:8], lhsT=b8, rhs=eye8_sb,
                                  start=True, stop=True))

            @blk.vector
            def _(v, cols=cols, scale=scale):
                s_pe.wait(v)
                s_dve.inc(v.tensor_scalar(out=cols[:], in0=P_MISC[:, 0:8], scalar1=scale,
                                scalar2=None, op0=ALU.mult))

        def emit_quant_gp(src_f32, dst_bf):
            @blk.vector
            def _(v, src_f32=src_f32):
                v.drain()
                x3 = src_f32.rearrange("p (b k) -> p b k", k=32)
                nb = E // 32
                v.tensor_reduce(out=q_amax[:], in_=x3, axis=AX.X, op=ALU.max,
                                apply_absolute_value=True)
                v.drain()
                v.tensor_scalar(out=q_sneg[:], in0=q_amax[:], scalar1=-1.0 / 7.5,
                                scalar2=-1e-30, op0=ALU.mult, op1=ALU.add)
                v.drain()
                v.tensor_scalar(out=q_spos[:], in0=q_sneg[:], scalar1=-1.0,
                                scalar2=None, op0=ALU.mult)
                v.reciprocal(out=q_nrs[:], in_=q_sneg[:])
                v.drain()
                nrs_b = q_nrs[:].unsqueeze(2).broadcast_to([128, nb, 32])
                v.tensor_tensor(out=q_y[:].rearrange("p (b k) -> p b k", k=32),
                                in0=x3, in1=nrs_b, op=ALU.mult)
                v.drain()
                v.tensor_scalar(out=q_m1[:].bitcast(U32), in0=q_y[:].bitcast(U32),
                                scalar1=0x7F800000, scalar2=None,
                                op0=ALU.bitwise_and)
                v.drain()
                s_dve.inc(v.tensor_scalar(out=q_m1[:], in0=q_m1[:], scalar1=1.0,
                                          scalar2=C_MAGIC, op0=ALU.max,
                                          op1=ALU.mult))

            @blk.gpsimd
            def _(g):
                s_dve.wait(g)
                g.tensor_tensor(out=q_t[:], in0=q_m1[:], in1=q_y[:], op=ALU.add)
                g.drain()
                s_st_gp.inc(g.tensor_tensor(out=q_qn[:], in0=q_m1[:], in1=q_t[:],
                                            op=ALU.subtract))

            @blk.vector
            def _(v):
                s_st_gp.wait(v)
                nb = E // 32
                spos_b = q_spos[:].unsqueeze(2).broadcast_to([128, nb, 32])
                s_dve.inc(v.tensor_tensor(
                    out=dst_bf.rearrange("p (b k) -> p b k", k=32),
                    in0=q_qn[:].rearrange("p (b k) -> p b k", k=32),
                    in1=spos_b, op=ALU.mult))
            return s_dve.n

        def emit_quant(v, src_f32, dst_bf):
            x3 = src_f32.rearrange("p (b k) -> p b k", k=32)
            nb = E // 32
            v.tensor_reduce(out=q_amax[:], in_=x3, axis=AX.X, op=ALU.max,
                            apply_absolute_value=True)
            v.drain()
            v.tensor_scalar(out=q_sneg[:], in0=q_amax[:], scalar1=-1.0 / 7.5,
                            scalar2=-1e-30, op0=ALU.mult, op1=ALU.add)
            v.drain()
            v.tensor_scalar(out=q_spos[:], in0=q_sneg[:], scalar1=-1.0,
                            scalar2=None, op0=ALU.mult)
            v.reciprocal(out=q_nrs[:], in_=q_sneg[:])
            v.drain()
            nrs_b = q_nrs[:].unsqueeze(2).broadcast_to([128, nb, 32])
            s_dve.inc(v.tensor_tensor(out=q_y[:].rearrange("p (b k) -> p b k", k=32),
                            in0=x3, in1=nrs_b, op=ALU.mult))
            v.tensor_scalar(out=q_m1[:].bitcast(U32), in0=q_y[:].bitcast(U32),
                            scalar1=0x7F800000, scalar2=None, op0=ALU.bitwise_and)
            s_dve.inc(v.tensor_scalar(out=q_m1[:], in0=q_m1[:], scalar1=1.0,
                            scalar2=C_MAGIC, op0=ALU.max, op1=ALU.mult))
            s_dve.inc(v.tensor_tensor(out=q_t[:], in0=q_m1[:], in1=q_y[:],
                            op=ALU.add))
            s_dve.inc(v.tensor_tensor(out=q_qn[:], in0=q_m1[:], in1=q_t[:],
                            op=ALU.subtract))
            spos_b = q_spos[:].unsqueeze(2).broadcast_to([128, nb, 32])
            return v.tensor_tensor(out=dst_bf.rearrange("p (b k) -> p b k", k=32),
                            in0=q_qn[:].rearrange("p (b k) -> p b k", k=32),
                            in1=spos_b, op=ALU.mult)

        w_plan = [("Wq", "qk", 0, 0.125, False), ("Wk", "qk", 1, 1.0, False),
                  ("Wv", "vg", 0, 1.0, False), ("Wg", "vg", 1, 1.0, False),
                  ("Wf", "wd", 0, 1.0, True)]
        Wc2_g = nc.sbuf_tensor("Wc2", [128, E], BF16)
        Wc2 = Wc2_g.__enter__()
        Ts2_g = nc.sbuf_tensor("Ts2", [128, E], BF16)
        Ts2 = Ts2_g.__enter__()
        cast_incs = [0, 0]
        tp_pe = [0, 0]
        sto_tok_w = {}
        gidx = 0
        for (wnm, kind, wi, cast_scale, do_quant) in w_plan:
            for ot in range(8):
                par = gidx % 2
                wcb = Wc_bf if par == 0 else Wc2
                tsb = Tstg if par == 0 else Ts2
                if do_quant:
                    wbuf, wkey = Wot, 'wot'
                    dve_war = s_dve.n
                else:
                    wbuf = Wot if par == 0 else q_m1
                    wkey = 'wot' if par == 0 else 'wot2'
                    dve_war = cast_incs[par]
                bld = LD([(s_dve, dve_war)],
                         (wbuf[:], Wd[wnm][ot * 128:(ot + 1) * 128, :]), key=wkey)

                @blk.vector
                def _(v, bld=bld, cast_scale=cast_scale, do_quant=do_quant,
                      wbuf=wbuf, wcb=wcb, pv=tp_pe[par]):
                    bld[0].wait(v, bld[1])
                    s_pe.wait(v, pv)
                    if do_quant:
                        s_dve.inc(emit_quant(v, wbuf[:], wcb[:]))
                    else:
                        s_dve.inc(v.tensor_scalar(out=wcb[:], in0=wbuf[:],
                                                  scalar1=cast_scale,
                                                  scalar2=None, op0=ALU.mult))
                cast_done = s_dve.n
                cast_incs[par] = cast_done

                pe_base = s_pe.n
                act_base = s_act.n

                @blk.tensor
                def _(t, cast_done=cast_done, act_base=act_base, wcb=wcb):
                    s_dve.wait(t, cast_done)
                    s_act.wait(t, act_base)
                    for kt in range(KT):
                        if kt >= 2:
                            s_act.wait(t, act_base + kt - 1)
                        pt = PT_bf if kt % 2 == 0 else PT2_bf
                        s_pe.inc(t.transpose(out=pt,
                                             in_=wcb[:, kt * 128:(kt + 1) * 128],
                                             identity=id_sb[:]))
                tp_pe[par] = s_pe.n

                def _w(sc, kind=kind, ot=ot, pe_base=pe_base, tsb=tsb,
                       tok=sto_tok_w.get(gidx - 2)):
                    if tok is not None:
                        tok[0].wait(sc, tok[1])
                    for kt in range(KT):
                        s_pe.wait(sc, pe_base + kt + 1)
                        pt = PT_bf if kt % 2 == 0 else PT2_bf
                        if kind == "wd":
                            dst = wdT[:, kt * E + ot * 128: kt * E + (ot + 1) * 128]
                        else:
                            dst = tsb[:, kt * 128:(kt + 1) * 128]
                        sc.copy(out=dst, in_=pt).then_inc(s_act.h, 1)
                SC(_w, KT)

                if kind == "qk":
                    sto_tok_w[gidx] = STO([(s_act, s_act.n)],
                                          (WTqk_scr[wi, ot], tsb[:]))
                elif kind == "vg":
                    sto_tok_w[gidx] = STO(
                        [(s_act, s_act.n)],
                        (WTvg_scr[wi][:, :, ot * 128:(ot + 1) * 128]
                         .rearrange("kt p o -> p kt o"),
                         tsb[:].rearrange("p (kt o) -> p kt o", o=128)))
                gidx += 1
        Ts2_g.__exit__(None, None, None)
        Wc2_g.__exit__(None, None, None)
        if "wdT" in dump_aps:
            STO([(s_act, s_act.n)], (dump_aps["wdT"], wdT[:]))
        flush_sc()
        P_MISC_g.__exit__(None, None, None)
        P_OUT2_g = nc.psum_tensor("P_OUT2", [128, 512], F32)
        P_OUT2 = P_OUT2_g.__enter__()

        def emit_bn_var(v, G, n_per_tile):
            n4 = n_per_tile / 4.0
            m = [bn_all[:, k:12 * G:12] for k in (1, 4, 7, 10)]
            M2 = [bn_all[:, k:12 * G:12] for k in (2, 5, 8, 11)]
            g = slice(0, G)
            v.tensor_tensor(out=st_m[:, g], in0=m[0], in1=m[1], op=ALU.add)
            v.tensor_tensor(out=st_dm[:, g], in0=m[2], in1=m[3], op=ALU.add)
            v.tensor_tensor(out=st_a[:, g], in0=m[0], in1=m[0], op=ALU.mult)
            v.tensor_tensor(out=st_h[:, g], in0=m[1], in1=m[1], op=ALU.mult)
            v.tensor_tensor(out=st_v[:, g], in0=M2[0], in1=M2[1], op=ALU.add)
            v.drain()
            v.tensor_tensor(out=st_m[:, g], in0=st_m[:, g], in1=st_dm[:, g],
                            op=ALU.add)
            v.tensor_tensor(out=st_a[:, g], in0=st_a[:, g], in1=st_h[:, g],
                            op=ALU.add)
            v.drain()
            v.tensor_tensor(out=st_h[:, g], in0=m[2], in1=m[2], op=ALU.mult)
            v.tensor_tensor(out=st_dm[:, g], in0=M2[2], in1=M2[3], op=ALU.add)
            v.tensor_scalar(out=st_m[:, g], in0=st_m[:, g], scalar1=0.25,
                            scalar2=None, op0=ALU.mult)
            v.drain()
            v.tensor_tensor(out=st_a[:, g], in0=st_a[:, g], in1=st_h[:, g],
                            op=ALU.add)
            v.tensor_tensor(out=st_v[:, g], in0=st_v[:, g], in1=st_dm[:, g],
                            op=ALU.add)
            v.drain()
            v.tensor_tensor(out=st_h[:, g], in0=m[3], in1=m[3], op=ALU.mult)
            v.tensor_tensor(out=st_dm[:, g], in0=st_m[:, g], in1=st_m[:, g],
                            op=ALU.mult)
            v.drain()
            v.tensor_tensor(out=st_a[:, g], in0=st_a[:, g], in1=st_h[:, g],
                            op=ALU.add)
            v.drain()
            v.scalar_tensor_tensor(out=st_v[:, g], in0=st_a[:, g], scalar=n4,
                                   in1=st_v[:, g], op0=ALU.mult, op1=ALU.add)
            v.drain()
            v.tensor_scalar(out=st_v[:, g], in0=st_v[:, g],
                            scalar1=1.0 / n_per_tile, scalar2=LN_EPS,
                            op0=ALU.mult, op1=ALU.add)
            v.drain()
            return v.tensor_tensor(out=st_v[:, g], in0=st_v[:, g],
                                   in1=st_dm[:, g], op=ALU.subtract)

        def emit_rstd(G):
            def _r(sc, dv=s_dve.n, G=G):
                s_dve.wait(sc, dv)
                sc.activation(out=st_h[:, 0:G], in_=st_v[:, 0:G], func=AF.Ln)
                sc.drain()
                sc.activation(out=st_r[:, 0:G], in_=st_h[:, 0:G], func=AF.Exp,
                              scale=-0.5)
                sc.drain().then_inc(s_act.h, 1)
            SC(_r)

            @blk.vector
            def _(v):
                s_act.wait(v)
                s_dve.inc(v.scalar_tensor_tensor(
                    out=st_nm[:, 0:G], in0=st_m[:, 0:G], scalar=-1.0,
                    in1=st_r[:, 0:G], op0=ALU.mult, op1=ALU.mult))
            return s_dve.n

        for b in range(n_batch):
            xnT_g = nc.sbuf_tensor("xnT_b%d" % b, [128, KT * T], BF16)
            xnT = xnT_g.__enter__()
            WvT_g = nc.sbuf_tensor("WvT_b%d" % b, [128, KT * E], BF16)
            WvT_b = WvT_g.__enter__()
            WgT_g = nc.sbuf_tensor("WgT_b%d" % b, [128, KT * E], BF16)
            WgT_b = WgT_g.__enter__()

            wvg_ld = LD(all_st(),
               (WvT_b[:].rearrange("p (kt o) -> p kt o", o=E),
                WTvg_scr[0].rearrange("kt p o -> p kt o")),
               (WgT_b[:].rearrange("p (kt o) -> p kt o", o=E),
                WTvg_scr[1].rearrange("kt p o -> p kt o")), key='wvg')

            xnT_done = {}
            pa_pe = {}
            bc_ev = {}
            idx_bc = [0]
            dv_b = s_dve.n
            for grp in range(NTT // 4):
                glds = []
                for j in range(4):
                    tt = grp * 4 + j
                    glds.append(LD([(s_dve, s_dve.n), (s_act, s_act.n)],
                                   (xt[j][:], x_in[b, tt * 128:(tt + 1) * 128, :]),
                                   key=('wot' if j == 0 else 'xt%d' % j)))

                @blk.vector
                def _(v, glds=glds):
                    v.drain()
                    for j in range(4):
                        glds[j][0].wait(v, glds[j][1])
                        v.bn_stats(out=bn_all[:, 12 * j:12 * j + 6],
                                   in_=xt[j][:, 0:512])
                        v.bn_stats(out=bn_all[:, 12 * j + 6:12 * j + 12],
                                   in_=xt[j][:, 512:1024])
                    v.drain()
                    s_dve.inc(emit_bn_var(v, 4, E))
                emit_rstd(4)
                rstdA = s_dve.n

                for j in range(4):
                    tt = grp * 4 + j
                    act_b = s_act.n
                    pe_b = s_pe.n

                    xnb = xn_tm if tt % 2 == 0 else Tstg
                    tokw = sto_tok_w.get(30) if (b == 0 and tt == 1) else None

                    def _n(sc, j=j, rstdA=rstdA, xnb=xnb, tok=tokw,
                           pw=pa_pe.get(tt - 2, pe_b)):
                        s_dve.wait(sc, rstdA)
                        s_pe.wait(sc, pw)
                        if tok is not None:
                            tok[0].wait(sc, tok[1])
                        sc.activation(out=xnb[:], in_=xt[j][:], func=AF.Identity,
                                      bias=st_nm[:, j:j + 1],
                                      scale=st_r[:, j:j + 1]).then_inc(s_act.h, 1)
                    SC(_n)

                    @blk.tensor
                    def _(t, act_b=act_b, xnb=xnb):
                        s_act.wait(t, act_b + 1)
                        for kt in range(KT):
                            if kt >= 2:
                                s_act.wait(t, act_b + 1 + kt - 1)
                            pt = PT_bf if kt % 2 == 0 else PT2_bf
                            s_pe.inc(t.transpose(out=pt,
                                                 in_=xnb[:, kt * 128:(kt + 1) * 128],
                                                 identity=id_sb[:]))
                    pa_pe[tt] = s_pe.n

                    def _e(sc, pe_b=pe_b, tt=tt, xnT=xnT):
                        for kt in range(KT):
                            s_pe.wait(sc, pe_b + kt + 1)
                            pt = PT_bf if kt % 2 == 0 else PT2_bf
                            sc.copy(out=xnT[:, kt * T + tt * 128:
                                            kt * T + (tt + 1) * 128],
                                    in_=pt).then_inc(s_act.h, 1)
                    SC(_e, KT)
                    xnT_done[tt] = s_act.n

                bcg = grp - 1
                if bcg >= 0:
                    for j in range(4):
                        tt = bcg * 4 + j
                        for (WT_b, brow, dstbuf, use_silu) in (
                                (WvT_b, bv_row, Vg, False),
                                (WgT_b, bg_row, gate, True)):
                            for half in range(2):
                                pbank = P_BIG if idx_bc[0] % 2 == 0 else P_S

                                @blk.tensor
                                def _(t, WT_b=WT_b, brow=brow, tt=tt, half=half,
                                      pbank=pbank, bw=bc_ev.get(idx_bc[0] - 2, 0),
                                      xd=xnT_done[tt]):
                                    s_act.wait(t, max(bw, xd))
                                    wvg_ld[0].wait(t, wvg_ld[1])
                                    for kt in range(KT):
                                        t.matmul(out=pbank[:],
                                                 lhsT=xnT[:, kt * T + tt * 128:
                                                          kt * T + (tt + 1) * 128],
                                                 rhs=WT_b[:, kt * E + half * 512:
                                                          kt * E + half * 512 + 512],
                                                 start=(kt == 0), stop=False)
                                    s_pe.inc(t.matmul(out=pbank[:], lhsT=ones1_sb[:],
                                                      rhs=brow[:, half * 512:
                                                               half * 512 + 512],
                                                      start=False, stop=True))

                                def _bc(sc, dstbuf=dstbuf, tt=tt, half=half,
                                        use_silu=use_silu, pv=s_pe.n, pbank=pbank,
                                        dv_b=dv_b):
                                    s_pe.wait(sc, pv)
                                    s_dve.wait(sc, dv_b)
                                    fn = AF.Silu if use_silu else AF.Copy
                                    sc.activation(out=dstbuf[:, tt * E + half * 512:
                                                             tt * E + half * 512 + 512],
                                                  in_=pbank[:],
                                                  func=fn).then_inc(s_act.h, 1)
                                SC(_bc)
                                bc_ev[idx_bc[0]] = s_act.n
                                idx_bc[0] += 1
            for bcg in [NTT // 4 - 1]:
                for j in range(4):
                    tt = bcg * 4 + j
                    for (WT_b, brow, dstbuf, use_silu) in (
                            (WvT_b, bv_row, Vg, False),
                            (WgT_b, bg_row, gate, True)):
                        for half in range(2):
                            pbank = P_BIG if idx_bc[0] % 2 == 0 else P_S

                            @blk.tensor
                            def _(t, WT_b=WT_b, brow=brow, tt=tt, half=half,
                                  pbank=pbank, bw=bc_ev.get(idx_bc[0] - 2, 0),
                                  xd=xnT_done[tt]):
                                s_act.wait(t, max(bw, xd))
                                wvg_ld[0].wait(t, wvg_ld[1])
                                for kt in range(KT):
                                    t.matmul(out=pbank[:],
                                             lhsT=xnT[:, kt * T + tt * 128:
                                                      kt * T + (tt + 1) * 128],
                                             rhs=WT_b[:, kt * E + half * 512:
                                                      kt * E + half * 512 + 512],
                                             start=(kt == 0), stop=False)
                                s_pe.inc(t.matmul(out=pbank[:], lhsT=ones1_sb[:],
                                                  rhs=brow[:, half * 512:
                                                           half * 512 + 512],
                                                  start=False, stop=True))

                            def _bc(sc, dstbuf=dstbuf, tt=tt, half=half,
                                    use_silu=use_silu, pv=s_pe.n, pbank=pbank,
                                    dv_b=dv_b):
                                s_pe.wait(sc, pv)
                                s_dve.wait(sc, dv_b)
                                fn = AF.Silu if use_silu else AF.Copy
                                sc.activation(out=dstbuf[:, tt * E + half * 512:
                                                         tt * E + half * 512 + 512],
                                              in_=pbank[:],
                                              func=fn).then_inc(s_act.h, 1)
                            SC(_bc)
                            bc_ev[idx_bc[0]] = s_act.n
                            idx_bc[0] += 1
            flush_sc()
            if "xnT" in dump_aps and b == 0:
                STO([(s_act, s_act.n)], (dump_aps["xnT"], xnT[:]))

            if "V" in dump_aps and b == 0:
                STO([(s_act, s_act.n)], (dump_aps["V"], Vg[:]))
            if "gate" in dump_aps and b == 0:
                STO([(s_act, s_act.n)], (dump_aps["gate"], gate[:]))

            WgT_g.__exit__(None, None, None)
            WvT_g.__exit__(None, None, None)
            hpD_g = []

            def hsb(name, shape, dt):
                gg = nc.sbuf_tensor(name + "_b%d" % b, shape, dt)
                tt_ = gg.__enter__()
                hpD_g.append(gg)
                return tt_

            K_tm2 = [hsb("K_tmA", [128, T], BF16), hsb("K_tmB", [128, T], BF16)]
            QrT2 = [QrT, hsb("QrT_B", [128, T], BF16)]
            KrT2 = [KrT, hsb("KrT_B", [128, T], BF16)]
            Wq2 = [Wq_hp, hsb("Wq_hp_B", [128, E], BF16)]
            Wk2 = [Wk_hp, hsb("Wk_hp_B", [128, E], BF16)]
            Wt2 = [Wt_hp, hsb("Wt_hp_B", [128, 256], BF16)]
            dmat2 = [dmat, hsb("dmat_B", [128, 128], BF16)]
            lam_pair2 = [lam_pair, hsb("lam_pair_B", [128, 2], BF16)]
            lam_mat2 = [lam_mat, hsb("lam_mat_B", [128, 128], BF16)]
            gamC2 = [gamC, hsb("gamC_B", [128, 1], F32)]
            gmix2 = [gmix, hsb("gmix_B", [128, 1], F32)]

            pt_free = {0: (s_act, s_act.n), 1: (s_act, s_act.n)}
            qbT_free = {c: (s_dve, s_dve.n) for c in range(4)}
            wbuf_free = {0: (s_pe, s_pe.n), 1: (s_pe, s_pe.n)}

            gmix_done = {}
            aux_done = {}
            ztn = {}
            init_done = {}

            def emit_gmix(hpn):
                parn = hpn % 2
                ha, hb_ = 2 * hpn, 2 * hpn + 1

                @blk.vector
                def _(v, parn=parn, ha=ha, hb_=hb_,
                      aw=aux_done.get(hpn - 2, 0)):
                    s_act.wait(v, aw)
                    v.tensor_copy(out=gmix2[parn][0:64, :],
                                  in_=negg_rep[0:64, ha:ha + 1])
                    s_dve.inc(v.tensor_copy(out=gmix2[parn][64:128, :],
                                  in_=negg_rep[64:128, hb_:hb_ + 1]))
                gmix_done[hpn] = s_dve.n

            def queue_aux(hpn):
                parn = hpn % 2
                ha, hb_ = 2 * hpn, 2 * hpn + 1

                def _aux(sc, parn=parn, ha=ha, hb_=hb_, gd=gmix_done[hpn]):
                    s_dve.wait(sc, gd)
                    sc.activation(out=dmat2[parn][:], in_=tau_sb[:], func=AF.Exp,
                                  scale=gmix2[parn][:, 0:1])
                    sc.activation(out=gamC2[parn][:], in_=c128_sb[:], func=AF.Exp,
                                  scale=gmix2[parn][:, 0:1])
                    sc.activation(out=lam_pair2[parn][:, 0:1], in_=cvec_sb[:],
                                  func=AF.Exp, scale=negg_rep[:, ha:ha + 1])
                    sc.activation(out=lam_pair2[parn][:, 1:2], in_=cvec_sb[:],
                                  func=AF.Exp, scale=negg_rep[:, hb_:hb_ + 1])
                    sc.activation(out=Wt2[parn][:, 0:128], in_=DT_sb[:], func=AF.Exp,
                                  scale=negg_rep[:, ha:ha + 1])
                    sc.activation(out=Wt2[parn][:, 128:256], in_=DT_sb[:],
                                  func=AF.Exp,
                                  scale=negg_rep[:, hb_:hb_ + 1]).then_inc(s_act.h, 1)
                SC(_aux)
                aux_done[hpn] = s_act.n

            def make_prep(hpn):
                parn = hpn % 2
                wldn = LD([wbuf_free[parn]],
                          (Wq2[parn][:], WTqk_scr[0, hpn]),
                          (Wk2[parn][:], WTqk_scr[1, hpn]), key='whp')
                pe = [[] for _ in range(16)]
                dve = [[] for _ in range(16)]
                sc_s = [[] for _ in range(16)]
                post_pe = []
                post_sc = []
                pq = {}
                pk = {}
                evq = {}
                evk = {}
                swq = {}
                swk = {}
                rotq = {}
                rotk = {}
                pT = {}
                ktm_done = {}
                st8 = {'pk3': None, 'ktm15': None}

                def bankof(c):
                    return P_T if c % 2 == 0 else P_T2

                def mk_proj(c, is_k):
                    def f(t, c=c, is_k=is_k):
                        wldn[0].wait(t, wldn[1])
                        pbank = bankof(c)
                        if not is_k:
                            if c < 2:
                                bf = pt_free[c % 2]
                                bf[0].wait(t, bf[1])
                            else:
                                s_dve.wait(t, rotq[c - 2])
                            qf = qbT_free.get(c)
                            if qf is not None:
                                qf[0].wait(t, qf[1])
                        else:
                            s_dve.wait(t, rotq[c + 2] if c < 2 else rotk[c - 2])
                        Wbuf = Wk2[parn] if is_k else Wq2[parn]
                        for kt in range(KT):
                            mm = t.matmul(out=pbank[:],
                                          lhsT=Wbuf[:, kt * 128:(kt + 1) * 128],
                                          rhs=xnT[:, kt * T + c * 512:
                                                  kt * T + (c + 1) * 512],
                                          start=(kt == 0), stop=(kt == KT - 1))
                        s_pe.inc(mm)
                        (pk if is_k else pq)[c] = s_pe.n
                    return f

                def mk_evac(c, is_k):
                    bcols = bk_cols if is_k else bq_cols

                    def f(sc, c=c, is_k=is_k, bcols=bcols):
                        s_pe.wait(sc, (pk if is_k else pq)[c])
                        if is_k:
                            s_dve.wait(sc, rotq[c])
                        sc.activation(out=QbT[:, c * 512:(c + 1) * 512],
                                      in_=bankof(c)[:], func=AF.Identity,
                                      bias=bcols[:, hpn:hpn + 1]).then_inc(s_act.h, 1)
                    return f

                def reg_evac(c, is_k):
                    SC(mk_evac(c, is_k))
                    (evk if is_k else evq)[c] = s_act.n

                def mk_swap(c, is_k):
                    def f(t, c=c, is_k=is_k):
                        s_act.wait(t, (evk if is_k else evq)[c])
                        s_pe.inc(t.matmul(out=bankof(c)[:], lhsT=P2_sb[:],
                                          rhs=QbT[:, c * 512:(c + 1) * 512],
                                          start=True, stop=True))
                        (swk if is_k else swq)[c] = s_pe.n
                    return f

                def mk_rot(c, is_k):
                    dst = KrT2[parn] if is_k else QrT2[parn]

                    def f(v, c=c, is_k=is_k, dst=dst):
                        s_pe.wait(v, (swk if is_k else swq)[c])
                        r1, r2 = rot_pairs[c % 2]
                        v.tensor_tensor(out=r1, in0=bankof(c)[:],
                                        in1=sin_sb[:, c * 512:(c + 1) * 512],
                                        op=ALU.mult)
                        v.tensor_tensor(out=r2, in0=QbT[:, c * 512:(c + 1) * 512],
                                        in1=cos_sb[:, c * 512:(c + 1) * 512],
                                        op=ALU.mult)
                        s_dve.inc(v.tensor_tensor(out=dst[:, c * 512:(c + 1) * 512],
                                        in0=r1, in1=r2, op=ALU.add))
                        (rotk if is_k else rotq)[c] = s_dve.n
                        if is_k:
                            qbT_free[c] = (s_dve, s_dve.n)
                    return f

                def mk_T(j):
                    def f(t, j=j):
                        if j < 2:
                            s_dve.wait(t, rotk[2 + j])
                        else:
                            s_act.wait(t, ktm_done[j - 2])
                        s_dve.wait(t, rotk[j // 4])
                        pt = PT_bf if j % 2 == 0 else PT2_bf
                        s_pe.inc(t.transpose(out=pt,
                                             in_=KrT2[parn][:, j * 128:(j + 1) * 128],
                                             identity=id_sb[:]))
                        pT[j] = s_pe.n
                    return f

                def mk_ktm(j):
                    def f(sc, j=j):
                        s_pe.wait(sc, pT[j])
                        pt = PT_bf if j % 2 == 0 else PT2_bf
                        sc.copy(out=K_tm2[parn][:, j * 128:(j + 1) * 128],
                                in_=pt).then_inc(s_act.h, 1)
                    return f

                def reg_ktm(j):
                    SC(mk_ktm(j))
                    ktm_done[j] = s_act.n
                    if j == 15:
                        pt_free[1] = (s_act, s_act.n)
                    if j == 14:
                        pt_free[0] = (s_act, s_act.n)

                pe[0].append(mk_proj(0, False))
                pe[1] += [mk_proj(1, False), mk_swap(0, False)]
                pe[2] += [mk_proj(2, False), mk_swap(1, False)]
                pe[3] += [mk_proj(3, False), mk_swap(2, False)]
                pe[4] += [mk_proj(0, True), mk_swap(3, False)]
                pe[5] += [mk_proj(1, True), mk_swap(0, True)]
                pe[6] += [mk_proj(2, True), mk_swap(1, True)]
                pe[7] += [mk_proj(3, True), mk_swap(2, True)]
                pe[8].append(mk_swap(3, True))
                for j in range(14):
                    pe[9 + j // 2].append(mk_T(j))
                post_pe += [mk_T(14), mk_T(15)]
                for c in range(4):
                    dve[1 + c].append(mk_rot(c, False))
                    dve[5 + c].append(mk_rot(c, True))
                sc_s[0].append(('aux', None))
                for c in range(4):
                    sc_s[c].append(('evac', (c, False)))
                    sc_s[4 + c].append(('evac', (c, True)))
                for j in range(14):
                    sc_s[9 + j // 2].append(('ktm', j))
                post_sc += [('ktm', 14), ('ktm', 15)]

                def run_sc(item):
                    kind, arg = item
                    if kind == 'aux':
                        queue_aux(hpn)
                    elif kind == 'evac':
                        reg_evac(*arg)
                    else:
                        reg_ktm(arg)

                def finish():
                    wbuf_free[parn] = (s_pe, pk[3])

                return dict(pe=pe, dve=dve, sc=sc_s, post_pe=post_pe,
                            post_sc=post_sc, run_sc=run_sc, finish=finish,
                            ktm_done=ktm_done, parn=parn)

            def emit_inter(hpn, prep):
                parn = hpn % 2

                @blk.vector
                def _(v):
                    s_dve.inc(v.memset(P_Z[:], 0.0))
                mz = s_dve.n
                if hpn + 1 < n_hp:
                    emit_gmix(hpn + 1)

                @blk.tensor
                def _(t, mz=mz, parn=parn, kd=prep['ktm_done'][15],
                      ha=2 * hpn, hb_=2 * hpn + 1):
                    t.wait_ge(s_dve.h, mz)
                    s_act.wait(t, kd)
                    for j in range(NCH):
                        t.matmul(out=P_Z[0:64, :],
                                 lhsT=K_tm2[parn][:, j * 128:j * 128 + 64],
                                 rhs=Vg[:, j * E + ha * 64:j * E + ha * 64 + 64],
                                 start=False, stop=False, skip_group_check=True,
                                 tile_position=(0, 0))
                        mm = t.matmul(out=P_Z[64:128, :],
                                      lhsT=K_tm2[parn][:, j * 128 + 64:(j + 1) * 128],
                                      rhs=Vg[:, j * E + hb_ * 64:j * E + hb_ * 64 + 64],
                                      start=False, stop=False, skip_group_check=True,
                                      tile_position=(0, 64))
                    s_pe.inc(mm)
                peY = s_pe.n

                def _zt(sc, peY=peY):
                    s_pe.wait(sc, peY)
                    sc.activation(out=Ztot[:], in_=P_Z[:], func=AF.Identity,
                                  scale=-1.0)
                    sc.activation(out=Zhi[:], in_=P_Z[:], func=AF.Identity,
                                  scale=-1.0).then_inc(s_act.h, 1)
                SC(_zt)
                ztn[hpn] = s_act.n

                @blk.vector
                def _(v, parn=parn, zt=ztn[hpn], ad=aux_done[hpn]):
                    v.drain()
                    s_act.wait(v, zt)
                    s_act.wait(v, ad)
                    v.memset(S_bfA[:], 0.0)
                    lam_b = lam_pair2[parn][:].unsqueeze(2).broadcast_to([128, 2, 64])
                    v.tensor_copy(out=lam_mat2[parn][:].rearrange(
                        "p (h d) -> p h d", d=64), in_=lam_b)
                    v.tensor_tensor(out=lamK2[0][:], in0=K_tm2[parn][:, 0:128],
                                    in1=lam_mat2[parn][:], op=ALU.mult)
                    v.tensor_tensor(out=Qt2[0][:], in0=QrT2[parn][:, 0:128],
                                    in1=dmat2[parn][:], op=ALU.mult)
                    s_dve.inc(v.tensor_tensor(out=Zlo[:], in0=Ztot[:],
                                              in1=Zhi[:], op=ALU.subtract))
                init_done[hpn] = s_dve.n

                @blk.tensor
                def _(t, idn=init_done[hpn]):
                    t.wait_ge(s_dve.h, idn)
                    t.matmul(out=P_Z[:], lhsT=id_sb[:], rhs=Zhi[:],
                             start=True, stop=False, skip_group_check=True)
                    t.matmul(out=P_Z[:], lhsT=id_sb[:], rhs=Zlo[:],
                             start=False, stop=False, skip_group_check=True)

            def chunk_weave(hp, prep_next):
                par = hp % 2
                ha, hb_ = 2 * hp, 2 * hp + 1
                K_tm_c, QrT_c = K_tm2[par], QrT2[par]
                KrT_c, Wt_c, gamC_c = KrT2[par], Wt2[par], gamC2[par]
                dmat_c, lam_mat_c = dmat2[par], lam_mat2[par]
                pe_st = {}
                pe_y = {}
                pe_u = {}
                pe_out = {}
                dWS = {}
                dSu = {}
                dTail = {}
                zd_val = {}

                for i in range(NCH):
                    o = i * 128
                    soff = 0 if i % 2 == 0 else 256
                    wv = dict(i=i, o=o, soff=soff)

                    @blk.tensor
                    def _(t, i=i, o=o, soff=soff, idn=init_done[hp],
                          zt=ztn[hp], pe_list=(prep_next['pe'][i]
                                               if prep_next else [])):
                        if i == 0:
                            t.wait_ge(s_dve.h, idn)
                            s_act.wait(t, zt)
                        else:
                            t.wait_ge(s_dve.h, dWS[i - 1])
                        t.matmul(out=P_S[:, soff:soff + 128],
                                 lhsT=KrT_c[0:64, o:o + 128],
                                 rhs=QrT_c[0:64, o:o + 128], start=True, stop=True,
                                 tile_position=(0, 0))
                        s_pe.inc(t.matmul(out=P_BIG[:, soff:soff + 128],
                                 lhsT=KrT_c[64:128, o:o + 128],
                                 rhs=QrT_c[64:128, o:o + 128], start=True, stop=True,
                                 tile_position=(64, 0)))
                        pe_st[i] = s_pe.n
                        if i >= 1:
                            t.wait_ge(s_act.h, zd_val[i - 1])
                        t.matmul(out=P_Z[0:64, :], lhsT=K_tm_c[:, o:o + 64],
                                 rhs=Vg[:, i * E + ha * 64:i * E + ha * 64 + 64],
                                 start=False, stop=False, skip_group_check=True,
                                 tile_position=(0, 0))
                        s_pe.inc(t.matmul(out=P_Z[64:128, :],
                                 lhsT=K_tm_c[:, o + 64:o + 128],
                                 rhs=Vg[:, i * E + hb_ * 64:i * E + hb_ * 64 + 64],
                                 start=False, stop=False, skip_group_check=True,
                                 tile_position=(0, 64)))
                        pe_y[i] = s_pe.n
                        if i >= 1:
                            t.wait_ge(s_dve.h, dSu[i - 1])
                            t.wait_ge(s_dve.h, dTail[i - 1])
                        t.matmul(out=P_U[0:64, :], lhsT=lamK2[i % 2][:, 0:64],
                                 rhs=Vg[:, i * E + ha * 64:i * E + ha * 64 + 64],
                                 start=True, stop=True, tile_position=(0, 0))
                        s_pe.inc(t.matmul(out=P_U[64:128, :],
                                 lhsT=lamK2[i % 2][:, 64:128],
                                 rhs=Vg[:, i * E + hb_ * 64:i * E + hb_ * 64 + 64],
                                 start=True, stop=True, tile_position=(0, 64)))
                        pe_u[i] = s_pe.n
                        for f in pe_list:
                            f(t)
                        if i >= 1:
                            j = i - 1
                            t.wait_ge(s_dve.h, dTail[j])
                            S_cur = S_bfA if j % 2 == 0 else S_bfB
                            WSc, Qtc = WS2[j % 2], Qt2[j % 2]
                            jo = j * 128
                            t.matmul(out=P_OUT[:, 0:64], lhsT=WSc[:, 0:128],
                                     rhs=Vg[:, j * E + ha * 64:j * E + ha * 64 + 64],
                                     start=True, stop=False)
                            t.matmul(out=P_OUT[:, 0:64], lhsT=Qtc[0:64, :],
                                     rhs=S_cur[0:64, :],
                                     start=False, stop=(j == NCH - 1),
                                     tile_position=(0, 0))
                            if j < NCH - 1:
                                t.matmul(out=P_OUT[:, 0:64],
                                         lhsT=QrT_c[0:64, jo:jo + 128],
                                         rhs=Zd2[j % 2][0:64, :],
                                         start=False, stop=True,
                                         tile_position=(0, 0))
                            t.matmul(out=P_OUT2[:, 0:64], lhsT=WSc[:, 128:256],
                                     rhs=Vg[:, j * E + hb_ * 64:j * E + hb_ * 64 + 64],
                                     start=True, stop=False)
                            t.matmul(out=P_OUT2[:, 0:64], lhsT=Qtc[64:128, :],
                                     rhs=S_cur[64:128, :],
                                     start=False, stop=(j == NCH - 1),
                                     tile_position=(64, 0))
                            if j < NCH - 1:
                                s_pe.inc(t.matmul(out=P_OUT2[:, 0:64],
                                         lhsT=QrT_c[64:128, jo:jo + 128],
                                         rhs=Zd2[j % 2][64:128, :],
                                         start=False, stop=True,
                                         tile_position=(64, 0)))
                            else:
                                s_pe.inc(t.nop())
                            pe_out[j] = s_pe.n

                    if i < NCH - 1:
                        def _zd(sc, i=i, py=pe_y[i]):
                            s_pe.wait(sc, py)
                            sc.activation(out=Zd2[i % 2][:], in_=P_Z[:],
                                          func=AF.Identity,
                                          scale=-1.0).then_inc(s_act.h, 1)
                        SC(_zd)
                        zd_val[i] = s_act.n
                    if prep_next:
                        for item in prep_next['sc'][i]:
                            prep_next['run_sc'](item)

                    @blk.vector
                    def _(v, i=i, soff=soff,
                          dve_list=(prep_next['dve'][i] if prep_next else [])):
                        if i == 0:
                            v.drain()
                        s_pe.wait(v, pe_st[i])
                        v.tensor_tensor(out=WS2[i % 2][:, 0:128],
                                        in0=P_S[:, soff:soff + 128],
                                        in1=Wt_c[:, 0:128], op=ALU.mult)
                        s_dve.inc(v.tensor_tensor(out=WS2[i % 2][:, 128:256],
                                        in0=P_BIG[:, soff:soff + 128],
                                        in1=Wt_c[:, 128:256], op=ALU.mult))
                        dWS[i] = s_dve.n
                        for f in dve_list:
                            f(v)
                        s_pe.wait(v, pe_u[i])
                        if i >= 1:
                            s_pe.wait(v, pe_out[i - 1])
                        S_cur = S_bfA if i % 2 == 0 else S_bfB
                        S_nxt = S_bfB if i % 2 == 0 else S_bfA
                        s_dve.inc(v.scalar_tensor_tensor(out=S_nxt[:], in0=S_cur[:],
                                               scalar=gamC_c[:, 0:1], in1=P_U[:],
                                               op0=ALU.mult, op1=ALU.add))
                        dSu[i] = s_dve.n
                        last = None
                        if i >= 1:
                            j = i - 1
                            last = v.tensor_tensor(
                                out=Vg[:, j * E + ha * 64:j * E + ha * 64 + 64],
                                in0=P_OUT[:, 0:64],
                                in1=gate[:, j * E + ha * 64:j * E + ha * 64 + 64],
                                op=ALU.mult)
                            last = v.tensor_tensor(
                                out=Vg[:, j * E + hb_ * 64:j * E + hb_ * 64 + 64],
                                in0=P_OUT2[:, 0:64],
                                in1=gate[:, j * E + hb_ * 64:j * E + hb_ * 64 + 64],
                                op=ALU.mult)
                        if i < NCH - 1:
                            o2 = (i + 1) * 128
                            v.tensor_tensor(out=lamK2[(i + 1) % 2][:],
                                            in0=K_tm_c[:, o2:o2 + 128],
                                            in1=lam_mat_c[:], op=ALU.mult)
                            last = v.tensor_tensor(out=Qt2[(i + 1) % 2][:],
                                            in0=QrT_c[:, o2:o2 + 128],
                                            in1=dmat_c[:], op=ALU.mult)
                        s_dve.inc(last)
                        dTail[i] = s_dve.n

                @blk.tensor
                def _(t, post=(prep_next['post_pe'] if prep_next else [])):
                    j = NCH - 1
                    jo = j * 128
                    t.wait_ge(s_dve.h, dTail[j])
                    S_cur = S_bfA if j % 2 == 0 else S_bfB
                    WSc, Qtc = WS2[j % 2], Qt2[j % 2]
                    t.matmul(out=P_OUT[:, 0:64], lhsT=WSc[:, 0:128],
                             rhs=Vg[:, j * E + ha * 64:j * E + ha * 64 + 64],
                             start=True, stop=False)
                    t.matmul(out=P_OUT[:, 0:64], lhsT=Qtc[0:64, :],
                             rhs=S_cur[0:64, :], start=False, stop=True,
                             tile_position=(0, 0))
                    t.matmul(out=P_OUT2[:, 0:64], lhsT=WSc[:, 128:256],
                             rhs=Vg[:, j * E + hb_ * 64:j * E + hb_ * 64 + 64],
                             start=True, stop=False)
                    s_pe.inc(t.matmul(out=P_OUT2[:, 0:64], lhsT=Qtc[64:128, :],
                             rhs=S_cur[64:128, :], start=False, stop=True,
                             tile_position=(64, 0)))
                    pe_out[NCH - 1] = s_pe.n
                    for f in post:
                        f(t)
                if prep_next:
                    for item in prep_next['post_sc']:
                        prep_next['run_sc'](item)

                @blk.vector
                def _(v):
                    j = NCH - 1
                    s_pe.wait(v, pe_out[j])
                    v.tensor_tensor(
                        out=Vg[:, j * E + ha * 64:j * E + ha * 64 + 64],
                        in0=P_OUT[:, 0:64],
                        in1=gate[:, j * E + ha * 64:j * E + ha * 64 + 64],
                        op=ALU.mult)
                    s_dve.inc(v.tensor_tensor(
                        out=Vg[:, j * E + hb_ * 64:j * E + hb_ * 64 + 64],
                        in0=P_OUT2[:, 0:64],
                        in1=gate[:, j * E + hb_ * 64:j * E + hb_ * 64 + 64],
                        op=ALU.mult))
                if prep_next:
                    prep_next['finish']()

            emit_gmix(0)
            prep = make_prep(0)
            order = []
            for s in range(16):
                order.append(('sc', prep['sc'][s]))
                order.append(('pe', prep['pe'][s]))
                order.append(('dve', prep['dve'][s]))
            for kind, items in order:
                for it in items:
                    if kind == 'sc':
                        prep['run_sc'](it)
                    elif kind == 'pe':
                        @blk.tensor
                        def _(t, it=it):
                            it(t)
                    else:
                        @blk.vector
                        def _(v, it=it):
                            v.drain()
                            it(v)
            for it in prep['post_pe']:
                @blk.tensor
                def _(t, it=it):
                    it(t)
            for item in prep['post_sc']:
                prep['run_sc'](item)
            prep['finish']()

            emit_inter(0, prep)
            for hp in range(n_hp):
                prep_next = make_prep(hp + 1) if hp + 1 < n_hp else None
                chunk_weave(hp, prep_next)
                if prep_next:
                    emit_inter(hp + 1, prep_next)

            for gg in reversed(hpD_g):
                gg.__exit__(None, None, None)

            if "attnout" in dump_aps and b == 0:
                STO([(s_dve, s_dve.n)], (dump_aps["attnout"], Vg[:]))

            xnT_g.__exit__(None, None, None)

            if not do_p2:
                continue

            p2_g = []

            def p2sb(name, shape, dt):
                gg = nc.sbuf_tensor(name + "_b%d" % b, shape, dt)
                tt_ = gg.__enter__()
                p2_g.append(gg)
                return tt_

            ffn_t = [p2sb("ffn_%d" % j, [128, E], F32) for j in range(5)]
            xg_t = [p2sb("xg_%d" % j, [128, E], F32) for j in range(2)]
            o1tmp = [p2sb("o1t_%d" % j, [128, E], F32) for j in range(2)]
            xdT_t = p2sb("xdT", [128, E], BF16)
            yo_t = [p2sb("yo_%d" % j, [128, E], F32) for j in range(2)]
            r2_r = p2sb("r2_r", [128, 16], F32)
            r2_nm = p2sb("r2_nm", [128, 16], F32)
            st_r4 = p2sb("st_r4", [128, 16], F32)
            st_nm4 = p2sb("st_nm4", [128, 16], F32)
            out1_all = gate

            def emit_bn_var2(v, G, n_per_tile, ib, so):
                n4 = n_per_tile / 4.0
                m = [bn_all[:, 12 * ib + k:12 * (ib + G):12] for k in (1, 4, 7, 10)]
                M2 = [bn_all[:, 12 * ib + k:12 * (ib + G):12] for k in (2, 5, 8, 11)]
                g = slice(so, so + G)
                v.tensor_tensor(out=st_m[:, g], in0=m[0], in1=m[1], op=ALU.add)
                v.tensor_tensor(out=st_dm[:, g], in0=m[2], in1=m[3], op=ALU.add)
                v.tensor_tensor(out=st_a[:, g], in0=m[0], in1=m[0], op=ALU.mult)
                v.tensor_tensor(out=st_h[:, g], in0=m[1], in1=m[1], op=ALU.mult)
                v.tensor_tensor(out=st_v[:, g], in0=M2[0], in1=M2[1], op=ALU.add)
                v.drain()
                v.tensor_tensor(out=st_m[:, g], in0=st_m[:, g], in1=st_dm[:, g],
                                op=ALU.add)
                v.tensor_tensor(out=st_a[:, g], in0=st_a[:, g], in1=st_h[:, g],
                                op=ALU.add)
                v.drain()
                v.tensor_tensor(out=st_h[:, g], in0=m[2], in1=m[2], op=ALU.mult)
                v.tensor_tensor(out=st_dm[:, g], in0=M2[2], in1=M2[3], op=ALU.add)
                v.tensor_scalar(out=st_m[:, g], in0=st_m[:, g], scalar1=0.25,
                                scalar2=None, op0=ALU.mult)
                v.drain()
                v.tensor_tensor(out=st_a[:, g], in0=st_a[:, g], in1=st_h[:, g],
                                op=ALU.add)
                v.tensor_tensor(out=st_v[:, g], in0=st_v[:, g], in1=st_dm[:, g],
                                op=ALU.add)
                v.drain()
                v.tensor_tensor(out=st_h[:, g], in0=m[3], in1=m[3], op=ALU.mult)
                v.tensor_tensor(out=st_dm[:, g], in0=st_m[:, g], in1=st_m[:, g],
                                op=ALU.mult)
                v.drain()
                v.tensor_tensor(out=st_a[:, g], in0=st_a[:, g], in1=st_h[:, g],
                                op=ALU.add)
                v.drain()
                v.scalar_tensor_tensor(out=st_v[:, g], in0=st_a[:, g], scalar=n4,
                                       in1=st_v[:, g], op0=ALU.mult, op1=ALU.add)
                v.drain()
                v.tensor_scalar(out=st_v[:, g], in0=st_v[:, g],
                                scalar1=1.0 / n_per_tile, scalar2=LN_EPS,
                                op0=ALU.mult, op1=ALU.add)
                v.drain()
                return v.tensor_tensor(out=st_v[:, g], in0=st_v[:, g],
                                       in1=st_dm[:, g], op=ALU.subtract)

            def emit_rstd2(so, do, dr, dnm):
                def _r(sc, dv=s_dve.n):
                    s_dve.wait(sc, dv)
                    sc.activation(out=st_h[:, so:so + 4], in_=st_v[:, so:so + 4],
                                  func=AF.Ln)
                    sc.drain()
                    sc.activation(out=dr[:, do:do + 4], in_=st_h[:, so:so + 4],
                                  func=AF.Exp, scale=-0.5)
                    sc.drain().then_inc(s_act.h, 1)
                SC(_r)

                @blk.vector
                def _(v, av=s_act.n):
                    s_act.wait(v, av)
                    s_dve.inc(v.scalar_tensor_tensor(
                        out=dnm[:, do:do + 4], in0=st_m[:, so:so + 4], scalar=-1.0,
                        in1=dr[:, do:do + 4], op0=ALU.mult, op1=ALU.mult))
                return s_dve.n

            @blk.vector
            def _(v):
                v.drain()
                for tt in range(NTT):
                    v.bn_stats(out=bn_all[:, 12 * tt:12 * tt + 6],
                               in_=Vg[:, tt * E:tt * E + 512])
                    v.bn_stats(out=bn_all[:, 12 * tt + 6:12 * tt + 12],
                               in_=Vg[:, tt * E + 512:(tt + 1) * E])
                v.drain()
                s_dve.inc(emit_bn_var(v, 16, E))
            emit_rstd(16)

            @blk.vector
            def _(v):
                v.drain()
                v.tensor_copy(out=r2_r[:], in_=st_r[:])
                s_dve.inc(v.tensor_copy(out=r2_nm[:], in_=st_nm[:]))

            p1_add = {}
            r3neg = {}
            r4neg = {}
            quantY = {}
            xd_done = {}
            pe_tp = {-1: s_pe.n}
            evx = {-1: s_act.n}
            ffn_ev2 = {}
            ffn_pe = {}
            FBANKS = [P_BIG, P_S, P_OUT, P_OUT2]
            sto_tok = {}
            dv_ffn0 = s_dve.n

            def emit_p1(tt):
                g = tt // 4
                xld = LD([(s_dve, max(p1_add.get(tt - 2, 0), dv_ffn0))],
                         (xg_t[tt % 2][:], x_in[b, tt * 128:(tt + 1) * 128, :]),
                         key='xg%d' % (tt % 2))

                @blk.vector
                def _(v, tt=tt, g=g, xld=xld):
                    xld[0].wait(v, xld[1])
                    v.scalar_tensor_tensor(out=o1tmp[tt % 2][:],
                                           in0=Vg[:, tt * E:(tt + 1) * E],
                                           scalar=r2_r[:, tt:tt + 1],
                                           in1=xg_t[tt % 2][:],
                                           op0=ALU.mult, op1=ALU.add)
                    s_dve.inc(v.tensor_scalar(
                        out=out1_all[:, tt * E:(tt + 1) * E],
                        in0=o1tmp[tt % 2][:], scalar1=r2_nm[:, tt:tt + 1],
                        scalar2=None, op0=ALU.add))
                    p1_add[tt] = s_dve.n
                    v.drain()
                    sl = 12 * (tt % 8)
                    v.bn_stats(out=bn_all[:, sl:sl + 6],
                               in_=out1_all[:, tt * E:tt * E + 512])
                    s_dve.inc(v.bn_stats(out=bn_all[:, sl + 6:sl + 12],
                               in_=out1_all[:, tt * E + 512:(tt + 1) * E]))
                    if tt % 4 == 3:
                        v.drain()
                        s_dve.inc(emit_bn_var2(v, 4, E, 4 * (g % 2), 4 * (g % 2)))
                if tt % 4 == 3:
                    r3neg[g] = emit_rstd2(4 * (g % 2), 4 * g, st_r, st_nm)

            x2a_d = {}

            def emit_p2x(tt):
                def _x2(sc, tt=tt):
                    s_dve.wait(sc, max(r3neg[tt // 4], quantY.get(tt - 1, 0)))
                    sc.activation(out=Wot[:], in_=out1_all[:, tt * E:(tt + 1) * E],
                                  func=AF.Identity, scale=st_r[:, tt:tt + 1],
                                  bias=st_nm[:, tt:tt + 1]).then_inc(s_act.h, 1)
                SC(_x2)
                x2a_d[tt] = s_act.n

            def emit_p2(tt):
                g = tt // 4
                x2a = x2a_d[tt]

                @blk.vector
                def _(v, tt=tt, x2a=x2a, pv=pe_tp[tt - 1]):
                    s_act.wait(v, x2a)
                    s_pe.wait(v, pv)
                    qb = s_dve.n
                    s_dve.inc(emit_quant(v, Wot[:], q_xd[:]))
                    quantY[tt] = qb + 1
                xd_done[tt] = s_dve.n

                evbase = s_act.n

                def _xe(sc, tt=tt):
                    pv = pe_tp[tt] - KT
                    for kt in range(KT):
                        s_pe.wait(sc, pv + kt + 1)
                        pt = PT_bf if kt % 2 == 0 else PT2_bf
                        sc.copy(out=xdT_t[:, kt * 128:(kt + 1) * 128],
                                in_=pt).then_inc(s_act.h, 1)
                SC(_xe, KT)
                evx[tt] = s_act.n

                @blk.tensor
                def _(t, tt=tt, xd=xd_done[tt], ev=evbase):
                    t.wait_ge(s_dve.h, xd)
                    s_act.wait(t, ev)
                    for kt in range(KT):
                        if kt >= 2:
                            s_act.wait(t, ev + kt - 1)
                        pt = PT_bf if kt % 2 == 0 else PT2_bf
                        s_pe.inc(t.transpose(out=pt,
                                             in_=q_xd[:, kt * 128:(kt + 1) * 128],
                                             identity=id_sb[:]))
                pe_tp[tt] = s_pe.n

                for half in range(2):
                    pbank = FBANKS[(2 * tt + half) % 4]

                    @blk.tensor
                    def _(t, half=half, pbank=pbank, av=evx[tt],
                          fv=ffn_ev2.get((tt - 2, half), 0)):
                        s_act.wait(t, max(av - (1 if half == 1 else 0), fv))
                        if tt <= 1:
                            t.wait_ge(s_dve.h, dv_ffn0)
                        for kt in range(KT):
                            t.matmul(out=pbank[:],
                                     lhsT=xdT_t[:, kt * 128:(kt + 1) * 128],
                                     rhs=wdT[:, kt * E + half * 512:
                                             kt * E + half * 512 + 512],
                                     start=(kt == 0), stop=False)
                        s_pe.inc(t.matmul(out=pbank[:], lhsT=ones1_sb[:],
                                          rhs=bf_row[:, half * 512:half * 512 + 512],
                                          start=False, stop=True))
                    ffn_pe[(tt, half)] = s_pe.n

            def emit_p2b(tt):
                g = tt // 4
                for half in range(2):
                    def _fe(sc, tt=tt, half=half, pv=ffn_pe[(tt, half)],
                            pbank=FBANKS[(2 * tt + half) % 4]):
                        s_pe.wait(sc, pv)
                        sc.copy(out=ffn_t[tt % 5][:, half * 512:half * 512 + 512],
                                in_=pbank[:]).then_inc(s_act.h, 1)
                    SC(_fe)
                    ffn_ev2[(tt, half)] = s_act.n

                @blk.vector
                def _(v, tt=tt, g=g, fv=ffn_ev2[(tt, 1)]):
                    s_act.wait(v, fv)
                    sl = 12 * (8 + tt % 8)
                    v.bn_stats(out=bn_all[:, sl:sl + 6],
                               in_=ffn_t[tt % 5][:, 0:512])
                    s_dve.inc(v.bn_stats(out=bn_all[:, sl + 6:sl + 12],
                               in_=ffn_t[tt % 5][:, 512:1024]))
                    if tt % 4 == 3:
                        v.drain()
                        s_dve.inc(emit_bn_var2(v, 4, E, 8 + 4 * (g % 2),
                                               8 + 4 * (g % 2)))
                if tt % 4 == 3:
                    r4neg[g] = emit_rstd2(8 + 4 * (g % 2), 4 * g, st_r4, st_nm4)

            def emit_p3(tt):
                g = tt // 4

                def _yo(sc, tt=tt, r4=r4neg[g], tok=sto_tok.get(tt - 2)):
                    s_dve.wait(sc, r4)
                    if tok is not None:
                        tok[0].wait(sc, tok[1])
                    sc.activation(out=yo_t[tt % 2][:], in_=ffn_t[tt % 5][:],
                                  func=AF.Identity, scale=st_r4[:, tt:tt + 1],
                                  bias=st_nm4[:, tt:tt + 1]).then_inc(s_act.h, 1)
                SC(_yo)
                yoa = s_act.n

                @blk.vector
                def _(v, tt=tt, yoa=yoa):
                    s_act.wait(v, yoa)
                    s_dve.inc(v.tensor_tensor(out=yo_t[tt % 2][:],
                                    in0=yo_t[tt % 2][:],
                                    in1=out1_all[:, tt * E:(tt + 1) * E],
                                    op=ALU.add))
                sto_tok[tt] = STO([(s_dve, s_dve.n)],
                                  (y_out[b, tt * 128:(tt + 1) * 128, :],
                                   yo_t[tt % 2][:]))

            LAG2, LAG2B, LAG3 = 4, 6, 9
            for step in range(NTT + LAG3):
                if step < NTT:
                    emit_p1(step)
                if 0 <= step - LAG2B < NTT:
                    emit_p2b(step - LAG2B)
                if 0 <= step - LAG3 < NTT:
                    emit_p3(step - LAG3)
                if 0 <= step - LAG2 < NTT:
                    emit_p2x(step - LAG2)
                    emit_p2(step - LAG2)
            flush_sc()

            for gg in reversed(p2_g):
                gg.__exit__(None, None, None)

        flush_sc()

        @blk.gpsimd
        def _(g):
            for p in ST_POOL:
                p.wait(g)
        P_OUT2_g.__exit__(None, None, None)

    return nc, dump_aps


def _host_consts():
    fi = 1.0 / 10000.0 ** (np.arange(0, D, 2, dtype=np.float64) / D)
    t = np.arange(T, dtype=np.float64)
    p = np.arange(128)
    ang = t[None, :] * fi[p % 32][:, None]
    cos2 = np.cos(ang).astype(ml_dtypes.bfloat16)
    sin2 = np.sin(ang).astype(ml_dtypes.bfloat16)
    P2 = np.zeros((128, 128), dtype=np.float32)
    for i in range(64):
        P2[2 * i + 1, 2 * i] = -1.0
        P2[2 * i, 2 * i + 1] = 1.0
    DT = np.maximum(np.arange(128)[None, :] - np.arange(128)[:, None], 0).astype(np.float32)
    tau = np.broadcast_to(np.arange(128, dtype=np.float32)[None, :], (128, 128)).copy()
    cvec = (float(C) - np.arange(128, dtype=np.float32))[:, None].copy()
    return dict(cos2=cos2, sin2=sin2, P2c=P2.astype(ml_dtypes.bfloat16),
                identc=np.eye(128, dtype=ml_dtypes.bfloat16), DTc=DT, tauc=tau,
                cvecc=cvec, c128c=np.full((128, 1), float(C), np.float32),
                ones1=np.ones((1, 128), np.float32),
                eye8=np.eye(8, dtype=np.float32))


def _make_base_inputs(L0_attn_Wq, L0_attn_bq, L0_attn_Wk, L0_attn_bk, L0_attn_Wv,
                      L0_attn_bv, L0_attn_Wg, L0_attn_bg, L0_attn_gamma,
                      L0_ffn_weight, L0_ffn_bias):
    base = dict(
        Wq=np.ascontiguousarray(L0_attn_Wq, np.float32),
        Wk=np.ascontiguousarray(L0_attn_Wk, np.float32),
        Wv=np.ascontiguousarray(L0_attn_Wv, np.float32),
        Wg=np.ascontiguousarray(L0_attn_Wg, np.float32),
        Wf=np.ascontiguousarray(L0_ffn_weight, np.float32),
        bq=np.ascontiguousarray(L0_attn_bq, np.float32),
        bk=np.ascontiguousarray(L0_attn_bk, np.float32),
        bv=np.ascontiguousarray(L0_attn_bv, np.float32),
        bg=np.ascontiguousarray(L0_attn_bg, np.float32),
        bf=np.ascontiguousarray(L0_ffn_bias, np.float32),
        gamma=np.ascontiguousarray(np.asarray(L0_attn_gamma, np.float32).reshape(1, H)),
    )
    base.update(_host_consts())
    return base


def kernel(L0_attn_x, L0_attn_Wq, L0_attn_bq, L0_attn_Wk, L0_attn_bk,
           L0_attn_Wv, L0_attn_bv, L0_attn_Wg, L0_attn_bg, L0_attn_gamma,
           L0_ffn_weight, L0_ffn_bias):
    x = np.ascontiguousarray(np.asarray(L0_attn_x, dtype=np.float32))
    if "full" not in _NC_CACHE:
        _NC_CACHE["full"] = build_core()
    nc, _ = _NC_CACHE["full"]
    base = _make_base_inputs(L0_attn_Wq, L0_attn_bq, L0_attn_Wk, L0_attn_bk,
                             L0_attn_Wv, L0_attn_bv, L0_attn_Wg, L0_attn_bg,
                             L0_attn_gamma, L0_ffn_weight, L0_ffn_bias)
    in_maps = []
    for c in range(NCORES):
        m = dict(base)
        m["x"] = x[c * B_PC:(c + 1) * B_PC]
        in_maps.append(m)
    res = run_bass_kernel_spmd(nc, in_maps, core_ids=list(range(NCORES)))
    out = np.empty((NCORES * B_PC, T, E), np.float32)
    for c in range(NCORES):
        out[c * B_PC:(c + 1) * B_PC] = np.asarray(res.results[c]["y"])
    return out
